# version 14
# baseline (speedup 1.0000x reference)
import numpy as np
import concourse.bass as bass
import concourse.mybir as mybir
from concourse.bass_utils import run_bass_kernel_spmd

F32 = mybir.dt.float32
BF16 = mybir.dt.bfloat16
I8 = mybir.dt.int8
AF = mybir.ActivationFunctionType
ALU = mybir.AluOpType
AX = mybir.AxisListType

SEQ, S, D, HID = 2, 2048, 1024, 4096
T = SEQ * S
GPS = S // 512

ENGS = ("sync", "tensor", "vector", "scalar", "gpsimd")
ENGMAP = {"q": "sync", "m": "tensor", "v": "vector", "s": "scalar", "g": "gpsimd"}


class Sem:
    def __init__(self, h):
        self.h = h
        self.n = 0


class P:
    def __init__(self, nc):
        self.nc = nc
        self.streams = {e: [] for e in ENGS}
        self.sem = {}
        for name in ("q", "g", "m", "v", "s"):
            cm = nc.semaphore(name)
            self.sem[name] = Sem(cm.__enter__())

    def wait(self, eng, semname, val):
        if val is None or val <= 0:
            return
        h = self.sem[semname].h
        self.streams[ENGMAP[eng]].append(lambda e, h=h, v=val: e.wait_ge(h, v))

    def op(self, eng, fn, semname=None, k=1):
        if semname is None:
            self.streams[ENGMAP[eng]].append(fn)
            return None
        sm = self.sem[semname]
        sm.n += k
        self.streams[ENGMAP[eng]].append(lambda e, fn=fn, h=sm.h, k=k: fn(e).then_inc(h, k))
        return sm.n

    def vchain(self, fn):
        n = self.op("v", fn, "v")
        self.wait("v", "v", n)
        return n

    def dma(self, eng, out, in_, semname):
        sm = self.sem[semname]
        sm.n += 16
        self.streams[ENGMAP[eng]].append(
            lambda e, o=out, i=in_, h=sm.h: e.dma_start(out=o, in_=i).then_inc(h, 16))
        return sm.n

    def run(self, blk):
        for eng in ENGS:
            stream = self.streams[eng]

            def body(e, stream=stream):
                for fn in stream:
                    fn(e)
            getattr(blk, eng)(body)


def build_nc(chain=("attn_a", "ffn", "attn_b", "ffn"), dbg=False):
    nc = bass.Bass("TRN2", target_bir_lowering=False)
    dram = {}

    def din(name, shape, dtype=F32):
        dram[name] = nc.dram_tensor(name, list(shape), dtype, kind="ExternalInput").ap()
        return dram[name]

    din("x_in", (T, D))
    y_out = nc.dram_tensor("y", [T, D], F32, kind="ExternalOutput").ap()

    for nm, sh in [("wqk", (D, 2048)), ("wv_a", (D, D)), ("wn", (D, 16)),
                   ("wo_a", (D, D)), ("wq_b", (D, D)), ("wk_b", (D, D)),
                   ("wv_b", (D, D)), ("wo_b", (D, D)), ("w1", (HID, D)),
                   ("w2", (D, HID)),
                   ("bqk_pp", (128, 16)), ("bn_pp", (16, 1)),
                   ("bq_b_pp", (128, 8)), ("bk_b_pp", (128, 8)),
                   ("b1_pp", (128, 32)), ("b2_row_f32", (1, 1024)),
                   ("rows_c", (128, 2048)),
                   ("ident_c", (128, 128)), ("cs_c", (128, 896)),
                   ("esel_c", (16, 1024)), ("omask_c", (128, 128)),
                   ("cvals", (128, 8))]:
        din(nm, sh)

    xsA = nc.dram_tensor("xsA", [T, D], F32, kind="Internal").ap()
    xsB = nc.dram_tensor("xsB", [T, D], F32, kind="Internal").ap()
    w1t_s = nc.dram_tensor("w1t_s", [D, HID], BF16, kind="Internal").ap()
    w2t_s = nc.dram_tensor("w2t_s", [HID, D], BF16, kind="Internal").ap()
    wot_s = nc.dram_tensor("wot_s", [D, D], BF16, kind="Internal").ap()

    ABC = 89216
    AFW = nc.alloc_sbuf_tensor([128, ABC], BF16)
    AFN = nc.alloc_sbuf_tensor([128, 5376], F32)
    H8t = nc.alloc_sbuf_tensor([128, 1024], I8)
    H8b = nc.alloc_sbuf_tensor([128, 1024], I8)
    WNs = nc.alloc_sbuf_tensor([128, 128], BF16)
    IDENT = nc.alloc_sbuf_tensor([128, 128], BF16)
    CSs = nc.alloc_sbuf_tensor([128, 896], BF16)
    ONES = nc.alloc_sbuf_tensor([128, 128], BF16)
    OMASK = nc.alloc_sbuf_tensor([128, 128], BF16)
    CV = nc.alloc_sbuf_tensor([128, 8], F32)
    BQK = nc.alloc_sbuf_tensor([128, 16], F32)
    BNn = nc.alloc_sbuf_tensor([16, 2], F32)
    BQB = nc.alloc_sbuf_tensor([128, 8], F32)
    BKB = nc.alloc_sbuf_tensor([128, 8], F32)
    B1Q = nc.alloc_sbuf_tensor([128, 64], F32)
    B1I = nc.alloc_sbuf_tensor([128, 32], I8)
    RWS = nc.alloc_sbuf_tensor([128, 3072], BF16)
    STAT = nc.alloc_sbuf_tensor([128, 32], F32)

    def abv(c0, c1, p0=0, p1=128):
        return AFW[p0:p1, c0:c1]

    def afv(c0, c1, p0=0, p1=128):
        return AFN[p0:p1, c0:c1]

    XT = [afv(0, 1024), afv(1024, 2048)]
    XR = [afv(2048, 3072), afv(3072, 4096)]
    XO = [afv(4096, 4608), afv(4608, 5120)]
    WFv = [afv(2048, 3072), afv(3072, 4096)]
    B2F = afv(4096, 5120, 0, 1)

    XN = [abv(0, 1024), abv(1024, 2048)]
    XNT0 = 2048
    SQD = abv(6144, 7168)

    WQS = abv(72704, 73728)
    ASM0 = 7168
    WSTG0 = 73728

    A_WQK, A_WV, A_WO = 7168, 23552, 31744
    A_QK, A_VA, A_VAO = 39936, 48128, 52224
    A_KTOK, A_EA, A_SSB = 56320, 58368, 60416
    A_CT, A_ENBC, A_ESEL, A_ENT = 61440, 65536, 67584, 68608
    F_W1, F_W2, F_HS = 7168, 39936, 72704
    B_WQ, B_WK, B_WV, B_WOT = 7168, 15360, 23552, 31744
    B_KT, B_VE = 39936, 56320
    B_QT, B_EB, B_EBM = 76800, 80896, 82944
    B_CT, B_BCB, B_RC = 83968, 88064, 88576
    assert B_RC + 512 <= ABC

    PS01 = nc.psum_tensor("ps01", [128, 1024], F32).__enter__()
    PS23 = nc.psum_tensor("ps23", [128, 1024], F32).__enter__()
    PS45 = nc.psum_tensor("ps45", [128, 1024], F32).__enter__()
    PS6 = nc.psum_tensor("ps6", [128, 512], F32).__enter__()
    PST = nc.psum_tensor("pst", [128, 1024], BF16).__enter__()

    p = P(nc)
    q, g, m, v, s = "q", "g", "m", "v", "s"

    def r3(ap_, **kw):
        return ap_.rearrange("p (a b) -> p a b", **kw)

    def recip_bf(e, out, in_):
        with nc.allow_low_precision(reason="recip feeds bf16 matmul rhs"):
            return e.reciprocal(out=out, in_=in_)

    p.dma(g, IDENT[:, :], dram["ident_c"], g)
    p.dma(g, CSs[:, :], dram["cs_c"], g)
    p.dma(g, abv(A_ESEL, A_ESEL + 1024, 0, 16), dram["esel_c"], g)
    p.dma(g, OMASK[:, :], dram["omask_c"], g)
    p.dma(g, RWS[:, 0:2048], dram["rows_c"], g)
    p.dma(q, CV[:, :], dram["cvals"], q)
    p.dma(q, BQK[:, :], dram["bqk_pp"], q)
    p.dma(q, BNn[0:16, 0:1], dram["bn_pp"], q)
    p.dma(q, BQB[:, :], dram["bq_b_pp"], q)
    p.dma(q, BKB[:, :], dram["bk_b_pp"], q)
    bias_q = p.dma(q, B1Q[:, 0:32], dram["b1_pp"], q)
    consts_g = p.sem[g].n
    p.op(g, lambda e: e.memset(ONES[:, :], 1.0), g)
    ones_g = p.sem[g].n

    p.wait(m, g, consts_g)

    p.wait(v, q, bias_q)
    p.vchain(lambda e: e.tensor_single_scalar(out=BNn[0:16, 1:2], in_=BNn[0:16, 0:1],
                                              scalar=-1.0, op=ALU.mult))
    p.vchain(lambda e: e.tensor_single_scalar(out=B1I[:, :], in_=B1Q[:, 0:32],
                                              scalar=64.0, op=ALU.mult))
    p.vchain(lambda e: e.tensor_single_scalar(out=B1Q[:, 32:64], in_=B1I[:, :],
                                              scalar=0.25, op=ALU.mult))
    ldb2 = p.dma(q, B2F, dram["b2_row_f32"], q)
    p.wait(v, q, ldb2)
    p.vchain(lambda e: e.tensor_single_scalar(out=H8t[0:1, 0:1024], in_=B2F,
                                              scalar=64.0, op=ALU.mult))
    p.vchain(lambda e: e.tensor_single_scalar(out=RWS[0:1, 2048:3072],
                                              in_=H8t[0:1, 0:1024],
                                              scalar=16.0, op=ALU.mult))
    b2_ready = p.sem[v].n

    wf_free = [0, 0]
    h8_free = [0]
    wqs_free = [0]

    def emit_wprep(src_dram, n_row_tiles, row_cols, dst_scratch, dst_rowlen, quantize,
                   pcount=[0]):
        nblk = row_cols // 128
        p.wait(s, g, consts_g)
        p.wait(q, v, b2_ready)
        for it in range(n_row_tiles):
            npieces = row_cols // 1024
            for pc in range(npieces):
                par = pcount[0] % 2
                pcount[0] += 1
                p.wait(q, s, wf_free[par])
                ld = p.dma(q, WFv[par],
                           src_dram[it * 128:(it + 1) * 128, pc * 1024:(pc + 1) * 1024], q)
                p.wait(s, q, ld)
                if quantize:
                    p.wait(s, v, h8_free[0])
                    ss = p.op(s, lambda e, par=par: e.activation(
                        out=H8t[:, :], in_=WFv[par], func=AF.Identity, scale=64.0,
                        bias=CV[:, 2:3]), s)
                    wf_free[par] = ss
                    p.wait(v, s, ss)
                    cp = p.op(v, lambda e: e.tensor_copy(WQS, H8t[:, :]), v)
                    h8_free[0] = cp
                    p.wait(m, v, cp)
                else:
                    p.wait(s, m, wqs_free[0])
                    ss = p.op(s, lambda e, par=par: e.activation(
                        out=WQS, in_=WFv[par], func=AF.Identity, scale=1.0,
                        bias=CV[:, 2:3]), s)
                    wf_free[par] = ss
                    p.wait(m, s, ss)
                for b in range(8):
                    p.op(m, lambda e, b=b: e.transpose(
                        PST[:, b * 128:(b + 1) * 128],
                        WQS[:, b * 128:(b + 1) * 128], IDENT[:, :]),
                        m if b == 7 else None)
                mv = p.sem[m].n
                wqs_free[0] = mv
                p.wait(v, m, mv)
                dst = AFW[:, ASM0:ASM0 + nblk * n_row_tiles * 128].rearrange(
                    "p (kc it j) -> p kc it j", kc=nblk, j=128)[:, pc * 8:pc * 8 + 8, it, :]
                ev = p.op(v, lambda e, dst=dst: e.tensor_copy(dst, PST[:, :]), v)
                p.wait(m, v, ev)
        evall = p.sem[v].n
        p.wait(q, v, evall)
        rowlen = n_row_tiles * 128
        last = 0
        for kc in range(nblk):
            last = p.dma(q, dst_scratch[kc * 128:(kc + 1) * 128, :],
                         abv(ASM0 + kc * rowlen, ASM0 + (kc + 1) * rowlen), q)
        p.wait(v, q, last)
        return p.sem[q].n

    if "ffn" in chain:
        emit_wprep(dram["w1"], 32, 1024, w1t_s, HID, True)
        emit_wprep(dram["w2"], 8, 4096, w2t_s, D, True)
    if "attn_b" in chain:
        emit_wprep(dram["wo_b"], 8, 1024, wot_s, D, False)

    aw_done = 0
    if "attn_a" in chain:
        p.wait(g, v, p.sem[v].n)
        p.wait(g, q, p.sem[q].n)
        for kc in range(8):
            p.dma(g, abv(A_WQK + kc * 2048, A_WQK + (kc + 1) * 2048),
                  dram["wqk"][kc * 128:(kc + 1) * 128, :], g)
            p.dma(g, abv(A_WV + kc * 1024, A_WV + (kc + 1) * 1024),
                  dram["wv_a"][kc * 128:(kc + 1) * 128, :], g)
            p.dma(g, abv(A_WO + kc * 1024, A_WO + (kc + 1) * 1024),
                  dram["wo_a"][kc * 128:(kc + 1) * 128, :], g)
            p.dma(g, WNs[:, kc * 16:(kc + 1) * 16],
                  dram["wn"][kc * 128:(kc + 1) * 128, :], g)
        aw_done = p.sem[g].n

    prep_g = p.sem[g].n
    prep_v = p.sem[v].n

    store_marks = {}

    def x_tile(src, t_idx):
        return src[t_idx * 128:(t_idx + 1) * 128, :]

    xt_free = [0, 0]
    xnt_guard = [0]
    xn_guard = [0, 0]
    xr_free = [0, 0]
    xo_free = [0, 0]

    xt_pre = {}

    def prefetch_x(src, stage_idx, gl, prev_stage):
        for tdx in (gl * 4, gl * 4 + 1):
            par = tdx % 2
            if prev_stage is not None:
                p.wait(q, g, store_marks.get((prev_stage, gl), 0))
            p.wait(q, v, xt_free[par])
            xt_pre[(stage_idx, tdx)] = p.dma(q, XT[par], x_tile(src, tdx), q)

    def ln_group(src, gl, quant, prev_stage, stage_idx):
        for hg in range(2):
            tiles = [gl * 4 + 2 * hg, gl * 4 + 2 * hg + 1]
            mz = p.op(v, lambda e: e.memset(STAT[:, 2:4], 0.0), v)
            stats_s = None
            for i, tdx in enumerate(tiles):
                par = (2 * hg + i) % 2
                ld = xt_pre.pop((stage_idx, tdx), None)
                if ld is None:
                    if prev_stage is not None:
                        p.wait(q, g, store_marks.get((prev_stage, gl), 0))
                    p.wait(q, v, xt_free[par])
                    ld = p.dma(q, XT[par], x_tile(src, tdx), q)
                p.wait(v, q, ld)
                p.op(v, lambda e, par=par, i=i: e.reduce_sum(
                    out=STAT[:, i:i + 1], in_=XT[par], axis=AX.X), v)
                p.wait(s, q, ld)
                p.wait(s, v, mz)
                stats_s = p.op(s, lambda e, par=par, i=i: e.activation(
                    out=SQD, in_=XT[par], func=AF.Square, bias=CV[:, 2:3],
                    accum_out=STAT[:, 2 + i:3 + i]), s)
            p.wait(v, s, stats_s)
            p.vchain(lambda e: e.tensor_single_scalar(out=STAT[:, 4:6], in_=STAT[:, 0:2],
                                                      scalar=1.0 / D, op=ALU.mult))
            p.op(v, lambda e: e.tensor_tensor(out=STAT[:, 6:8], in0=STAT[:, 4:6],
                                              in1=STAT[:, 4:6], op=ALU.mult))
            p.vchain(lambda e: e.tensor_single_scalar(out=STAT[:, 2:4], in_=STAT[:, 2:4],
                                                      scalar=1.0 / D, op=ALU.mult))
            p.vchain(lambda e: e.tensor_tensor(out=STAT[:, 8:10], in0=STAT[:, 2:4],
                                               in1=STAT[:, 6:8], op=ALU.subtract))
            p.vchain(lambda e: e.tensor_single_scalar(out=STAT[:, 10:12], in_=STAT[:, 8:10],
                                                      scalar=1e-5, op=ALU.add))
            vr = p.op(v, lambda e: e.reciprocal(out=STAT[:, 6:8], in_=STAT[:, 10:12]), v)
            p.wait(s, v, vr)
            sr = p.op(s, lambda e: e.activation(out=STAT[:, 8:10], in_=STAT[:, 6:8],
                                                func=AF.Sqrt, bias=CV[:, 2:3]), s)
            p.wait(v, s, sr)
            if quant:
                p.vchain(lambda e: e.tensor_single_scalar(out=STAT[:, 12:14],
                                                          in_=STAT[:, 8:10],
                                                          scalar=16.0, op=ALU.mult))
                p.vchain(lambda e: e.tensor_tensor(out=STAT[:, 14:16], in0=STAT[:, 4:6],
                                                   in1=STAT[:, 12:14], op=ALU.mult))
                acol, bcol = 12, 14
            else:
                p.vchain(lambda e: e.tensor_tensor(out=STAT[:, 12:14], in0=STAT[:, 4:6],
                                                   in1=STAT[:, 8:10], op=ALU.mult))
                acol, bcol = 8, 12
            for i in range(2):
                ti = 2 * hg + i
                par = ti % 2
                p.wait(v, m, xn_guard[par])
                if quant:
                    p.op(v, lambda e, par=par, i=i: e.tensor_scalar(
                        out=H8t[:, :], in0=XT[par],
                        scalar1=STAT[:, acol + i:acol + i + 1],
                        scalar2=STAT[:, bcol + i:bcol + i + 1],
                        op0=ALU.mult, op1=ALU.subtract))
                    ap_done = p.op(v, lambda e, par=par: e.tensor_copy(
                        XN[par], H8t[:, :]), v)
                else:
                    ap_done = p.op(v, lambda e, par=par, i=i: e.tensor_scalar(
                        out=XN[par], in0=XT[par],
                        scalar1=STAT[:, acol + i:acol + i + 1],
                        scalar2=STAT[:, bcol + i:bcol + i + 1],
                        op0=ALU.mult, op1=ALU.subtract), v)
                xt_free[par] = ap_done
                p.wait(m, v, ap_done)
                for b in range(8):
                    p.op(m, lambda e, b=b, par=par: e.transpose(
                        PST[:, b * 128:(b + 1) * 128],
                        XN[par][:, b * 128:(b + 1) * 128], IDENT[:, :]),
                        m if b == 7 else None)
                tm = p.sem[m].n
                xn_guard[par] = tm
                p.wait(v, m, tm)
                if ti == 0:
                    p.wait(v, m, xnt_guard[0])
                dst = AFW[:, XNT0:XNT0 + 4096].rearrange(
                    "p (kc tt b) -> p kc tt b", kc=8, tt=4)[:, :, ti, :]
                ev = p.op(v, lambda e, dst=dst: e.tensor_copy(dst, PST[:, :]), v)
                p.wait(m, v, ev)
        return p.sem[v].n

    def xnt(kc, cols=512, off=0):
        return abv(XNT0 + kc * 512 + off, XNT0 + kc * 512 + off + cols)

    ffn_n = [0]

    def ffn(src, dst, stage_idx, prev_stage):
        first = ffn_n[0] == 0
        ffn_n[0] += 1
        prefetch_x(src, stage_idx, 0, prev_stage)
        p.wait(q, m, p.sem[m].n)
        p.wait(q, v, p.sem[v].n)
        p.wait(q, g, p.sem[g].n)
        p.wait(q, q, p.sem[q].n)
        w1_ready = 0
        for kc in range(8):
            w1_ready = p.dma(q, abv(F_W1 + kc * 4096, F_W1 + (kc + 1) * 4096),
                             w1t_s[kc * 128:(kc + 1) * 128, :], q)
        w_ready = w1_ready
        for hc in range(32):
            w_ready = p.dma(q, abv(F_W2 + hc * 1024, F_W2 + (hc + 1) * 1024),
                            w2t_s[hc * 128:(hc + 1) * 128, :], q)
        hs_guard = [0]
        p2_ev = [0, 0]
        for gl in range(T // 512):
            ln_group(src, gl, True, prev_stage, stage_idx)
            xnt_ready = p.sem[v].n
            p.wait(m, v, xnt_ready)
            p.wait(m, q, w1_ready)
            if first and gl == 0:
                p.wait(m, v, b2_ready)
            banks = [(PS01, 0), (PS01, 512), (PS23, 0), (PS23, 512)]
            ev_s = {}
            ev_v = {}
            for hc in range(32):
                bk, co = banks[hc % 4]
                p.wait(m, s, ev_s.get(hc - 4, 0))
                for kc in range(8):
                    p.op(m, lambda e, bk=bk, co=co, kc=kc, hc=hc: e.matmul(
                        bk[:, co:co + 512],
                        abv(F_W1 + kc * 4096 + hc * 128, F_W1 + kc * 4096 + hc * 128 + 128),
                        xnt(kc), start=(kc == 0), stop=(kc == 7)),
                        m if kc == 7 else None)
                mm = p.sem[m].n
                p.wait(s, m, mm)
                p.wait(s, v, ev_v.get(hc - 2, 0))
                sv = p.op(s, lambda e, bk=bk, co=co, hc=hc: e.activation(
                    out=H8b[:, (hc % 2) * 512:(hc % 2) * 512 + 512],
                    in_=bk[:, co:co + 512], func=AF.Identity, scale=1.0 / 64,
                    bias=B1Q[:, 32 + hc:33 + hc]), s)
                ev_s[hc] = sv
                p.wait(v, s, sv)
                if hc == 0:
                    p.wait(v, m, hs_guard[0])
                ev_v[hc] = p.op(v, lambda e, hc=hc: e.tensor_single_scalar(
                    out=abv(F_HS + hc * 512, F_HS + (hc + 1) * 512),
                    in_=H8b[:, (hc % 2) * 512:(hc % 2) * 512 + 512],
                    scalar=0.0, op=ALU.max), v)
            hs_ready = p.sem[v].n
            p.wait(m, v, hs_ready)
            xnt_guard[0] = p.sem[m].n
            if gl == 0:
                p.wait(m, q, w_ready)
            for tt in range(4):
                t_idx = gl * 4 + tt
                par = tt % 2
                p.wait(q, v, xr_free[par])
                ldr = p.dma(q, XR[par], x_tile(src, t_idx), q)
                for half in range(2):
                    co2 = half * 512
                    p.wait(m, v, p2_ev[half])
                    for hc in range(32):
                        p.op(m, lambda e, co2=co2, hc=hc, tt=tt, half=half: e.matmul(
                            PS45[:, co2:co2 + 512],
                            abv(F_HS + hc * 512 + tt * 128, F_HS + hc * 512 + tt * 128 + 128),
                            abv(F_W2 + hc * 1024 + half * 512, F_W2 + hc * 1024 + half * 512 + 512),
                            start=(hc == 0), stop=False), None)
                    mm = p.op(m, lambda e, co2=co2, half=half: e.matmul(
                        PS45[:, co2:co2 + 512], ONES[0:1, 0:128],
                        RWS[0:1, 2048 + half * 512:2048 + half * 512 + 512],
                        start=False, stop=True), m)
                    p.wait(v, m, mm)
                    p.wait(v, q, ldr)
                    p.wait(v, g, xo_free[half])
                    ev = p.op(v, lambda e, co2=co2, par=par, half=half: e.scalar_tensor_tensor(
                        out=XO[half], in0=PS45[:, co2:co2 + 512], scalar=1.0 / 1024,
                        in1=XR[par][:, half * 512:half * 512 + 512],
                        op0=ALU.mult, op1=ALU.add), v)
                    p.wait(g, v, ev)
                    xo_free[half] = p.dma(
                        g, dst[t_idx * 128:(t_idx + 1) * 128, half * 512:(half + 1) * 512],
                        XO[half], g)
                    p2_ev[half] = ev
                xr_free[par] = p.sem[v].n
            hs_guard[0] = p.sem[m].n
            store_marks[(stage_idx, gl)] = p.sem[g].n

    def attn_a(src, dst, stage_idx, prev_stage):
        a_opj = [0, 0]
        p.wait(g, v, prep_v)
        p.op(g, lambda e: e.memset(abv(A_VAO, A_VAO + 4096), 0.0), g)
        vao_init = p.sem[g].n
        slab_guard = [0]
        for sq in range(SEQ):
            p.wait(g, m, slab_guard[0])
            p.op(g, lambda e: e.memset(abv(A_SSB, A_SSB + 1024), 0.0), g)
            zdone = p.sem[g].n
            for gi in range(GPS):
                gl = sq * GPS + gi
                ln_group(src, gl, False, prev_stage, stage_idx)
                xnt_ready = p.sem[v].n
                p.wait(m, v, xnt_ready)
                if sq == 0 and gi == 0:
                    p.wait(m, g, aw_done)
                    p.wait(m, g, prep_g)
                p.wait(m, v, a_opj[0])
                pj = [(PS01, 0), (PS01, 512), (PS6, 0)]
                qk_ev = {}
                for mc in range(16):
                    bk, co = pj[mc % 3]
                    p.wait(m, s, qk_ev.get(mc - 3, 0))
                    if mc == 0:
                        p.wait(s, m, slab_guard[0])
                    for kc in range(8):
                        p.op(m, lambda e, bk=bk, co=co, mc=mc, kc=kc: e.matmul(
                            bk[:, co:co + 512],
                            abv(A_WQK + kc * 2048 + mc * 128, A_WQK + kc * 2048 + mc * 128 + 128),
                            xnt(kc), start=(kc == 0), stop=(kc == 7)),
                            m if kc == 7 else None)
                    mm = p.sem[m].n
                    p.wait(s, m, mm)
                    qk_ev[mc] = p.op(s, lambda e, bk=bk, co=co, mc=mc: e.activation(
                        out=abv(A_QK + mc * 512, A_QK + (mc + 1) * 512),
                        in_=bk[:, co:co + 512], func=AF.Identity, scale=1.0,
                        bias=BQK[:, mc:mc + 1]), s)
                p.wait(m, v, a_opj[1])
                for kc in range(8):
                    p.op(m, lambda e, kc=kc: e.matmul(
                        PS23[0:16, 0:512], WNs[:, kc * 16:(kc + 1) * 16],
                        xnt(kc), start=(kc == 0), stop=(kc == 7)),
                        m if kc == 7 else None)
                mm = p.sem[m].n
                p.wait(s, m, mm)
                en_done = p.op(s, lambda e, gi=gi: e.activation(
                    out=abv(A_ENT + gi * 512, A_ENT + (gi + 1) * 512, 0, 16),
                    in_=PS23[0:16, 0:512], func=AF.Exp, scale=-1.0,
                    bias=BNn[0:16, 1:2]), s)
                p.wait(m, s, en_done)
                p.wait(m, g, ones_g)
                p.wait(m, g, vao_init)
                va_ev = [0, 0]
                for tt in range(4):
                    for half in range(2):
                        co2 = 512 * ((tt * 2 + half) % 2)
                        p.wait(m, v, va_ev[co2 // 512])
                        for kc in range(8):
                            p.op(m, lambda e, co2=co2, tt=tt, kc=kc, half=half: e.matmul(
                                PS23[:, co2:co2 + 512], xnt(kc, 128, tt * 128),
                                abv(A_WV + kc * 1024 + half * 512, A_WV + kc * 1024 + half * 512 + 512),
                                start=(kc == 0), stop=False), None)
                        mm = p.op(m, lambda e, co2=co2, half=half: e.matmul(
                            PS23[:, co2:co2 + 512], ONES[0:1, 0:128],
                            RWS[0:1, half * 512:half * 512 + 512],
                            start=False, stop=True), m)
                        p.wait(v, m, mm)
                        if tt == 0 and half == 0:
                            p.wait(v, m, slab_guard[0])
                        p.op(v, lambda e, co2=co2, tt=tt, half=half: e.tensor_copy(
                            abv(A_VA + tt * 1024 + half * 512, A_VA + tt * 1024 + half * 512 + 512),
                            PS23[:, co2:co2 + 512]))
                        dst_odd = r3(abv(A_VAO + tt * 1024 + half * 512,
                                         A_VAO + tt * 1024 + half * 512 + 512), a=4)[:, :, 64:128]
                        ev = p.op(v, lambda e, co2=co2, dst_odd=dst_odd: e.tensor_copy(
                            dst_odd, r3(PS23[:, co2:co2 + 512], a=4)[:, :, 64:128]), v)
                        va_ev[co2 // 512] = ev
                p.wait(m, v, max(va_ev[0], va_ev[1]))
                xnt_guard[0] = p.sem[m].n
                p.wait(m, g, zdone)
                ktok_guard = [0, 0]
                for c in range(4):
                    ci = gi * 4 + c
                    kpar = c % 2
                    epar = c % 2
                    for fc in range(8):
                        p.op(m, lambda e, fc=fc, c=c: e.transpose(
                            PST[:, fc * 128:(fc + 1) * 128],
                            abv(A_QK + (8 + fc) * 512 + c * 128,
                                A_QK + (8 + fc) * 512 + c * 128 + 128),
                            IDENT[:, :]), m if fc == 7 else None)
                    tm = p.sem[m].n
                    p.wait(v, m, tm)
                    p.wait(v, m, ktok_guard[kpar])
                    ktok_ev = p.op(v, lambda e, kpar=kpar: e.tensor_copy(
                        abv(A_KTOK + kpar * 1024, A_KTOK + (kpar + 1) * 1024),
                        PST[:, :]), v)
                    p.wait(m, v, ktok_ev)
                    for half in range(2):
                        for ff in range(4):
                            fc = half * 4 + ff
                            p.op(m, lambda e, fc=fc, ff=ff, gi=gi, c=c: e.matmul(
                                PS6[:, ff * 128:(ff + 1) * 128],
                                abv(A_ESEL + fc * 128, A_ESEL + (fc + 1) * 128, 0, 16),
                                abv(A_ENT + gi * 512 + c * 128,
                                    A_ENT + gi * 512 + c * 128 + 128, 0, 16),
                                start=True, stop=True), m if ff == 3 else None)
                        mm = p.sem[m].n
                        p.wait(v, m, mm)
                        enb_ev = p.op(v, lambda e, epar=epar, half=half: e.tensor_copy(
                            abv(A_ENBC + epar * 1024 + half * 512,
                                A_ENBC + epar * 1024 + (half + 1) * 512),
                            PS6[:, 0:512]), v)
                        p.wait(m, v, enb_ev)
                    if ci > 0:
                        for bi in range(2):
                            p.op(v, lambda e, bi=bi: e.tensor_copy(
                                r3(abv(A_SSB + bi * 512, A_SSB + (bi + 1) * 512, 0, 64),
                                   a=4)[:, :, 0:64],
                                r3(PS45[0:64, bi * 512:(bi + 1) * 512], a=4)[:, :, 0:64]))
                            p.op(v, lambda e, bi=bi: e.tensor_copy(
                                r3(abv(A_SSB + bi * 512, A_SSB + (bi + 1) * 512, 64, 128),
                                   a=4)[:, :, 64:128],
                                r3(PS45[64:128, bi * 512:(bi + 1) * 512], a=4)[:, :, 64:128]), v)
                        ssb_ev = p.sem[v].n
                        p.wait(m, v, ssb_ev)
                    for half in range(2):
                        for ff in range(4):
                            fc = half * 4 + ff
                            kbase = A_QK + (8 + fc) * 512 + c * 128
                            qbase = A_QK + fc * 512 + c * 128
                            p.op(m, lambda e, kbase=kbase, qbase=qbase, ff=ff: e.matmul(
                                PS01[:, ff * 128:(ff + 1) * 128],
                                abv(kbase, kbase + 128, 0, 64),
                                abv(qbase, qbase + 128, 0, 64),
                                start=True, stop=True), None)
                            p.op(m, lambda e, kbase=kbase, qbase=qbase, ff=ff: e.matmul(
                                PS01[:, 512 + ff * 128:512 + (ff + 1) * 128],
                                abv(kbase, kbase + 128, 64, 128),
                                abv(qbase, qbase + 128, 64, 128),
                                start=True, stop=True), m if ff == 3 else None)
                        mm = p.sem[m].n
                        p.wait(v, m, mm)
                        mrep = bass.AP(CSs, 384, [[896, 128], [0, 4], [1, 128]])
                        p.op(v, lambda e, half=half, mrep=mrep: e.tensor_tensor(
                            out=r3(abv(A_EA + half * 1024, A_EA + half * 1024 + 512), a=4),
                            in0=r3(PS01[:, 0:512], a=4), in1=mrep, op=ALU.mult))
                        ea_ev = p.op(v, lambda e, half=half, mrep=mrep: e.tensor_tensor(
                            out=r3(abv(A_EA + half * 1024 + 512, A_EA + half * 1024 + 1024), a=4),
                            in0=r3(PS01[:, 512:1024], a=4), in1=mrep, op=ALU.mult), v)
                        p.wait(m, v, ea_ev)
                    for fc in range(8):
                        cbk = PS23
                        co = (0 if fc < 4 else 512) + (fc % 4) * 128
                        qbase = A_QK + fc * 512 + c * 128
                        eabase = A_EA + (0 if fc < 4 else 1024) + (fc % 4) * 128
                        started = False
                        if ci > 0:
                            p.op(m, lambda e, cbk=cbk, co=co, fc=fc, qbase=qbase: e.matmul(
                                cbk[:, co:co + 128],
                                abv(A_SSB + fc * 128, A_SSB + (fc + 1) * 128),
                                abv(qbase, qbase + 128),
                                start=True, stop=False), None)
                            started = True
                        p.op(m, lambda e, cbk=cbk, co=co, fc=fc, c=c, eabase=eabase,
                             st=(not started): e.matmul(
                            cbk[:, co:co + 128],
                            abv(A_VAO + c * 1024 + fc * 128, A_VAO + c * 1024 + (fc + 1) * 128),
                            abv(eabase + 512, eabase + 512 + 128),
                            start=st, stop=False), None)
                        p.op(m, lambda e, cbk=cbk, co=co, fc=fc, c=c, eabase=eabase: e.matmul(
                            cbk[0:64, co:co + 128],
                            abv(A_VA + c * 1024 + fc * 128, A_VA + c * 1024 + fc * 128 + 64),
                            abv(eabase, eabase + 128),
                            start=False, stop=True), None)
                        sco = (0 if fc < 4 else 512) + (fc % 4) * 128
                        p.op(m, lambda e, sco=sco, fc=fc, c=c, kpar=kpar, ci=ci: e.matmul(
                            PS45[:, sco:sco + 128],
                            abv(A_KTOK + kpar * 1024 + fc * 128, A_KTOK + kpar * 1024 + (fc + 1) * 128),
                            abv(A_VA + c * 1024 + fc * 128, A_VA + c * 1024 + (fc + 1) * 128),
                            start=(ci == 0 and fc % 4 == 0), stop=(ci == S // 128 - 1)),
                            m if fc == 7 else None)
                    mm = p.sem[m].n
                    ktok_guard[kpar] = mm
                    p.wait(v, m, mm)
                    for bi in range(2):
                        dst2 = AFW[:, A_CT:A_CT + 4096].rearrange(
                            "p (fc tt b) -> p fc tt b", fc=8, tt=4)[:, bi * 4:(bi + 1) * 4, c, :]
                        p.op(v, lambda e, dst2=dst2, epar=epar, bi=bi: e.tensor_tensor(
                            out=dst2, in0=r3(PS23[:, bi * 512:(bi + 1) * 512], a=4),
                            in1=r3(abv(A_ENBC + epar * 1024 + bi * 512,
                                       A_ENBC + epar * 1024 + (bi + 1) * 512), a=4),
                            op=ALU.mult), v)
                    ctx_ev = p.sem[v].n
                    p.wait(m, v, ctx_ev)
                for tt in range(4):
                    t_idx = gl * 4 + tt
                    par = tt % 2
                    p.wait(q, v, xr_free[par])
                    ldr = p.dma(q, XR[par], x_tile(src, t_idx), q)
                    for half in range(2):
                        cpar = (tt * 2 + half) % 2
                        ACC = PS6[:, 0:512] if cpar == 0 else PS23[:, 0:512]
                        p.wait(m, v, a_opj[cpar])
                        for fc in range(8):
                            p.op(m, lambda e, fc=fc, tt=tt, half=half, ACC=ACC: e.matmul(
                                ACC,
                                abv(A_CT + fc * 512 + tt * 128, A_CT + fc * 512 + tt * 128 + 128),
                                abv(A_WO + fc * 1024 + half * 512, A_WO + fc * 1024 + half * 512 + 512),
                                start=(fc == 0), stop=False), None)
                        mm = p.op(m, lambda e, half=half, ACC=ACC: e.matmul(
                            ACC, ONES[0:1, 0:128],
                            RWS[0:1, 1024 + half * 512:1024 + half * 512 + 512],
                            start=False, stop=True), m)
                        p.wait(v, m, mm)
                        p.wait(v, q, ldr)
                        p.wait(v, g, xo_free[half])
                        ev = p.op(v, lambda e, par=par, half=half, ACC=ACC: e.tensor_tensor(
                            out=XO[half], in0=XR[par][:, half * 512:half * 512 + 512],
                            in1=ACC, op=ALU.add), v)
                        p.wait(g, v, ev)
                        xo_free[half] = p.dma(
                            g, dst[t_idx * 128:(t_idx + 1) * 128,
                                   half * 512:(half + 1) * 512], XO[half], g)
                        a_opj[cpar] = ev
                    xr_free[par] = p.sem[v].n
                store_marks[(stage_idx, gl)] = p.sem[g].n
                slab_guard[0] = p.sem[m].n

    def attn_b(src, dst, stage_idx, prev_stage):
        eb_guard = [0, 0]
        ebm_guard = [0]
        exp_done = [0, 0]
        ctev = [0]
        evac_done = [0]
        bcb_done = [0]
        norm_pend = [None]
        p.wait(g, m, p.sem[m].n)
        p.wait(g, v, p.sem[v].n)
        wqkv_ready = 0
        for kc in range(8):
            p.dma(g, abv(B_WQ + kc * 1024, B_WQ + (kc + 1) * 1024),
                  dram["wq_b"][kc * 128:(kc + 1) * 128, :], g)
            p.dma(g, abv(B_WK + kc * 1024, B_WK + (kc + 1) * 1024),
                  dram["wk_b"][kc * 128:(kc + 1) * 128, :], g)
            wqkv_ready = p.dma(g, abv(B_WV + kc * 1024, B_WV + (kc + 1) * 1024),
                               dram["wv_b"][kc * 128:(kc + 1) * 128, :], g)
        wot_ready = 0
        prefetch_x(src, stage_idx, 0, prev_stage)
        p.wait(q, m, p.sem[m].n)
        p.wait(q, g, p.sem[g].n)
        p.wait(q, q, p.sem[q].n)
        for kc in range(8):
            wot_ready = p.dma(q, abv(B_WOT + kc * 1024, B_WOT + (kc + 1) * 1024),
                              wot_s[kc * 128:(kc + 1) * 128, :], q)
        p.op(g, lambda e: e.memset(abv(B_VE, B_VE + 20480), 0.0), g)
        ones_cols = AFW[:, B_VE:B_VE + 20480].rearrange(
            "p (n blk) -> p n blk", blk=160)[:, :, 64:65]
        ve_init = p.op(g, lambda e, oc=ones_cols: e.memset(oc, 1.0), g)

        slab_guard = [0]
        for sq in range(SEQ):
            for gi in range(GPS):
                gl = sq * GPS + gi
                qc = gi
                ln_group(src, gl, False, prev_stage, stage_idx)
                xnt_ready = p.sem[v].n
                p.wait(m, v, xnt_ready)
                p.wait(m, g, wqkv_ready)
                p.wait(m, g, ve_init)
                p.wait(s, m, slab_guard[0])
                p.wait(m, s, exp_done[1])
                pj_ev = [0, 0]
                for fc in range(8):
                    for which in ("q", "k"):
                        co2 = 0 if which == "q" else 512
                        wbase = (B_WQ if which == "q" else B_WK) + fc * 128
                        p.wait(m, s, pj_ev[co2 // 512])
                        for kc in range(8):
                            p.op(m, lambda e, co2=co2, wbase=wbase, kc=kc: e.matmul(
                                PS45[:, co2:co2 + 512],
                                abv(wbase + kc * 1024, wbase + kc * 1024 + 128),
                                xnt(kc), start=(kc == 0), stop=(kc == 7)),
                                m if kc == 7 else None)
                        mm = p.sem[m].n
                        p.wait(s, m, mm)
                        if which == "q":
                            dstq = abv(B_QT + fc * 512, B_QT + (fc + 1) * 512)
                            bias = BQB[:, fc:fc + 1]
                        else:
                            dstq = abv(B_KT + fc * 2048 + gi * 512,
                                       B_KT + fc * 2048 + (gi + 1) * 512)
                            bias = BKB[:, fc:fc + 1]
                        sv = p.op(s, lambda e, co2=co2, dstq=dstq, bias=bias: e.activation(
                            out=dstq, in_=PS45[:, co2:co2 + 512], func=AF.Identity,
                            scale=1.0, bias=bias), s)
                        pj_ev[co2 // 512] = sv
                p.wait(m, s, max(pj_ev[0], pj_ev[1]))
                vv_ev = [pj_ev[0], pj_ev[1]]
                vv_is_s = [True, True]
                for tt in range(4):
                    kt = gi * 4 + tt
                    for half in range(2):
                        co2 = 512 * ((tt * 2 + half) % 2)
                        if vv_is_s[co2 // 512]:
                            p.wait(m, s, vv_ev[co2 // 512])
                        else:
                            p.wait(m, v, vv_ev[co2 // 512])
                        for kc in range(8):
                            p.op(m, lambda e, co2=co2, tt=tt, kc=kc, half=half: e.matmul(
                                PS45[:, co2:co2 + 512], xnt(kc, 128, tt * 128),
                                abv(B_WV + kc * 1024 + half * 512, B_WV + kc * 1024 + half * 512 + 512),
                                start=(kc == 0), stop=False), None)
                        mm = p.op(m, lambda e, co2=co2, half=half: e.matmul(
                            PS45[:, co2:co2 + 512], ONES[64:65, 0:128],
                            RWS[64:65, half * 512:half * 512 + 512],
                            start=False, stop=True), m)
                        p.wait(v, m, mm)
                        if tt == 0 and half == 0:
                            p.wait(v, m, slab_guard[0])
                        base = B_VE + kt * 1280 + half * 640
                        dste = AFW[:, base:base + 640].rearrange(
                            "p (a blk) -> p a blk", blk=160)
                        srcp = PS45[:, co2:co2 + 512].rearrange("p (a h b) -> p a h b", a=4, h=2)
                        p.op(v, lambda e, dste=dste, srcp=srcp: e.tensor_copy(
                            dste[:, :, 0:64], srcp[:, :, 0, :]))
                        ev = p.op(v, lambda e, dste=dste, srcp=srcp: e.tensor_copy(
                            dste[:, :, 96:160], srcp[:, :, 1, :]), v)
                        vv_ev[co2 // 512] = ev
                        vv_is_s[co2 // 512] = False
                last_v = max(vv_ev[0], vv_ev[1])
                p.wait(m, v, last_v)
                xnt_guard[0] = p.sem[m].n
                nkt = 4 * qc + 4

                def emit_ctx(kt, sv, qc=qc):
                    diag = kt >= 4 * qc
                    ebpar = kt % 2
                    ebdst = abv(B_EB + ebpar * 1024, B_EB + (ebpar + 1) * 1024)
                    if diag:
                        c0 = 384 - (kt * 128 - qc * 512)
                        p.wait(v, s, sv)
                        p.wait(v, m, ebm_guard[0])
                        msk = bass.AP(CSs, c0, [[896, 128], [0, 2], [1, 512]])
                        mev = p.op(v, lambda e, ebdst=ebdst, msk=msk: e.tensor_tensor(
                            out=r3(abv(B_EBM, B_EBM + 1024), a=2),
                            in0=r3(ebdst, a=2), in1=msk, op=ALU.mult), v)
                        p.wait(m, v, mev)
                        esrc = abv(B_EBM, B_EBM + 1024)
                    else:
                        p.wait(m, s, sv)
                        esrc = ebdst
                    vbase = B_VE + kt * 1280 + fc * 160
                    p.op(m, lambda e, vbase=vbase, esrc=esrc, kt=kt: e.matmul(
                        PS23[:, 0:512], abv(vbase, vbase + 128), esrc[:, 0:512],
                        start=(kt == 0), stop=(kt == nkt - 1)), None)
                    mm2 = p.op(m, lambda e, vbase=vbase, esrc=esrc, kt=kt: e.matmul(
                        PS23[:, 512:1024], abv(vbase + 32, vbase + 160), esrc[:, 512:1024],
                        start=(kt == 0), stop=(kt == nkt - 1)), m)
                    eb_guard[kt % 2] = mm2
                    if diag:
                        ebm_guard[0] = mm2

                def norm_chain():
                    fc_p, rcc = norm_pend[0]
                    norm_pend[0] = None
                    slot = fc_p % 2
                    p.wait(m, v, rcc)
                    p.wait(m, v, bcb_done[0])
                    b1 = p.op(m, lambda e, slot=slot: e.matmul(
                        PS6[:, 0:512], OMASK[64:65, 0:128],
                        abv(slot * 1024, slot * 1024 + 512, 64, 65),
                        start=True, stop=False), m)
                    p.wait(m, m, b1)
                    bmm = p.op(m, lambda e, slot=slot: e.matmul(
                        PS6[:, 0:512], OMASK[32:33, 0:128],
                        abv(slot * 1024, slot * 1024 + 512, 32, 33),
                        start=False, stop=True), m)
                    p.wait(v, m, bmm)
                    bcb_done[0] = p.op(v, lambda e: e.tensor_copy(
                        abv(B_BCB, B_BCB + 512), PS6[:, 0:512]), v)
                    p.op(v, lambda e, fc_p=fc_p: e.tensor_tensor(
                        out=abv(B_CT + fc_p * 512, B_CT + (fc_p + 1) * 512, 0, 64),
                        in0=afv(4096, 4608, 0, 64),
                        in1=abv(B_BCB, B_BCB + 512, 0, 64), op=ALU.mult))
                    ctev[0] = p.op(v, lambda e, fc_p=fc_p: e.tensor_tensor(
                        out=abv(B_CT + fc_p * 512, B_CT + (fc_p + 1) * 512, 64, 128),
                        in0=afv(4096, 4608, 64, 128),
                        in1=abv(B_BCB, B_BCB + 512, 64, 128), op=ALU.mult), v)

                for fc in range(8):
                    pend = []
                    for kt in range(nkt):
                        spar = kt % 2
                        SCB = PS01 if spar == 0 else PS45
                        p.wait(m, s, exp_done[spar])
                        if kt == 0:
                            p.wait(m, v, evac_done[0])
                        if kt == nkt - 1 and norm_pend[0] is not None:
                            norm_chain()
                        kbase = B_KT + fc * 2048 + kt * 128
                        p.op(m, lambda e, kbase=kbase, fc=fc, SCB=SCB: e.matmul(
                            SCB[:, 0:512], abv(kbase, kbase + 128, 0, 64),
                            abv(B_QT + fc * 512, B_QT + (fc + 1) * 512, 0, 64),
                            start=True, stop=True), None)
                        mm = p.op(m, lambda e, kbase=kbase, fc=fc, SCB=SCB: e.matmul(
                            SCB[:, 512:1024], abv(kbase, kbase + 128, 64, 128),
                            abv(B_QT + fc * 512, B_QT + (fc + 1) * 512, 64, 128),
                            start=True, stop=True), m)
                        if len(pend) == 2:
                            emit_ctx(*pend.pop(0))
                        p.wait(s, m, mm)
                        p.wait(s, m, eb_guard[kt % 2])
                        ebdst = abv(B_EB + (kt % 2) * 1024, B_EB + (kt % 2 + 1) * 1024)
                        sv = p.op(s, lambda e, ebdst=ebdst, SCB=SCB: e.activation(
                            out=ebdst, in_=SCB[:, 0:1024], func=AF.Exp,
                            scale=0.125, bias=CV[:, 1:2]), s)
                        exp_done[spar] = sv
                        pend.append((kt, sv))
                    for pe in pend:
                        emit_ctx(*pe)
                    mmend = p.sem[m].n
                    p.wait(v, m, mmend)
                    slot = fc % 2
                    if fc == 0:
                        p.wait(v, g, max(xo_free[0], xo_free[1]))
                    p.op(v, lambda e: e.tensor_copy(
                        afv(4096, 4608, 0, 64), PS23[0:64, 0:512]))
                    p.op(v, lambda e: e.tensor_copy(
                        afv(4608, 5120, 64, 65), PS23[64:65, 0:512]))
                    p.op(v, lambda e: e.tensor_copy(
                        afv(4608, 5120, 32, 33), PS23[32:33, 512:1024]))
                    ev2 = p.op(v, lambda e: e.tensor_copy(
                        afv(4096, 4608, 64, 128), PS23[64:128, 512:1024]), v)
                    evac_done[0] = ev2
                    p.op(v, lambda e, slot=slot: recip_bf(
                        e, abv(slot * 1024, slot * 1024 + 512, 64, 65),
                        afv(4608, 5120, 64, 65)))
                    rc = p.op(v, lambda e, slot=slot: recip_bf(
                        e, abv(slot * 1024, slot * 1024 + 512, 32, 33),
                        afv(4608, 5120, 32, 33)), v)
                    norm_pend[0] = (fc, rc)
                p.wait(m, q, wot_ready)
                b_opj = [evac_done[0], bcb_done[0]]
                ct7_waited = [False]
                for tt in range(4):
                    t_idx = gl * 4 + tt
                    par = tt % 2
                    p.wait(q, v, xr_free[par])
                    ldr = p.dma(q, XR[par], x_tile(src, t_idx), q)
                    for half in range(2):
                        cpar = (tt * 2 + half) % 2
                        ACC = PS23[:, 0:512] if cpar == 0 else PS6[:, 0:512]
                        p.wait(m, v, b_opj[cpar])
                        for fc in range(8):
                            if fc == 7 and not ct7_waited[0]:
                                if norm_pend[0] is not None:
                                    norm_chain()
                                    b_opj[1] = max(b_opj[1], bcb_done[0])
                                p.wait(m, v, ctev[0])
                                ct7_waited[0] = True
                            p.op(m, lambda e, fc=fc, tt=tt, half=half, ACC=ACC: e.matmul(
                                ACC,
                                abv(B_CT + fc * 512 + tt * 128, B_CT + fc * 512 + tt * 128 + 128),
                                abv(B_WOT + fc * 1024 + half * 512, B_WOT + fc * 1024 + half * 512 + 512),
                                start=(fc == 0), stop=False), None)
                        mm = p.op(m, lambda e, half=half, ACC=ACC: e.matmul(
                            ACC, ONES[64:65, 0:128],
                            RWS[64:65, 1024 + half * 512:1024 + half * 512 + 512],
                            start=False, stop=True), m)
                        p.wait(v, m, mm)
                        p.wait(v, q, ldr)
                        p.wait(v, g, xo_free[half])
                        ev = p.op(v, lambda e, par=par, half=half, ACC=ACC: e.tensor_tensor(
                            out=XO[half], in0=XR[par][:, half * 512:half * 512 + 512],
                            in1=ACC, op=ALU.add), v)
                        p.wait(g, v, ev)
                        xo_free[half] = p.dma(
                            g, dst[t_idx * 128:(t_idx + 1) * 128,
                                   half * 512:(half + 1) * 512], XO[half], g)
                        b_opj[cpar] = ev
                    xr_free[par] = p.sem[v].n
                evac_done[0] = b_opj[0]
                bcb_done[0] = b_opj[1]
                store_marks[(stage_idx, gl)] = p.sem[g].n
                slab_guard[0] = p.sem[m].n

    cur = dram["x_in"]
    scratch = [xsA, xsB]
    for i, kind in enumerate(chain):
        dst = y_out if i == len(chain) - 1 else scratch[i % 2]
        prev = i - 1 if i > 0 else None
        if kind == "ffn":
            ffn(cur, dst, i, prev)
        elif kind == "attn_a":
            attn_a(cur, dst, i, prev)
        elif kind == "attn_b":
            attn_b(cur, dst, i, prev)
        cur = dst

    if dbg:
        d1 = nc.dram_tensor("dbg_xnt", [128, 4096], F32, kind="ExternalOutput").ap()
        d2 = nc.dram_tensor("dbg_w1t", [128, 4096], F32, kind="ExternalOutput").ap()
        d3 = nc.dram_tensor("dbg_hs", [128, 16384], F32, kind="ExternalOutput").ap()
        d4 = nc.dram_tensor("dbg_w2t", [128, 1024], F32, kind="ExternalOutput").ap()
        p.wait(v, m, p.sem[m].n)
        p.wait(v, g, p.sem[g].n)
        p.wait(v, q, p.sem[q].n)
        p.wait(v, s, p.sem[s].n)
        p.wait(q, g, p.sem[g].n)
        p.wait(q, v, p.sem[v].n)
        p.wait(q, q, p.sem[q].n)
        for blk_i in range(4):
            cp = p.op(v, lambda e, blk_i=blk_i: e.tensor_copy(
                XT[0], abv(XNT0 + blk_i * 1024, XNT0 + (blk_i + 1) * 1024)), v)
            p.wait(g, v, cp)
            wr = p.dma(g, d1[:, blk_i * 1024:(blk_i + 1) * 1024], XT[0], g)
            p.wait(v, g, wr)
        ld = p.dma(q, abv(WSTG0, WSTG0 + 4096), w1t_s[0:128, :], q)
        p.wait(v, q, ld)
        for blk_i in range(4):
            cp = p.op(v, lambda e, blk_i=blk_i: e.tensor_copy(
                XT[0], abv(WSTG0 + blk_i * 1024, WSTG0 + (blk_i + 1) * 1024)), v)
            p.wait(g, v, cp)
            wr = p.dma(g, d2[:, blk_i * 1024:(blk_i + 1) * 1024], XT[0], g)
            p.wait(v, g, wr)
        for blk_i in range(16):
            cp = p.op(v, lambda e, blk_i=blk_i: e.tensor_copy(
                XT[0], abv(F_HS + blk_i * 1024, F_HS + (blk_i + 1) * 1024)), v)
            p.wait(g, v, cp)
            wr = p.dma(g, d3[:, blk_i * 1024:(blk_i + 1) * 1024], XT[0], g)
            p.wait(v, g, wr)
        ld = p.dma(q, abv(WSTG0, WSTG0 + 1024), w2t_s[0:128, :], q)
        p.wait(v, q, ld)
        cp = p.op(v, lambda e: e.tensor_copy(XT[0], abv(WSTG0, WSTG0 + 1024)), v)
        p.wait(g, v, cp)
        p.dma(g, d4[:, 0:1024], XT[0], g)

    p.wait(g, g, p.sem[g].n)
    p.wait(g, q, p.sem[q].n)
    p.wait(g, v, p.sem[v].n)
    p.wait(g, s, p.sem[s].n)
    p.wait(g, m, p.sem[m].n)

    with nc.Block() as blk:
        p.run(blk)
    return nc


def _host_inputs(core, full):
    d = {}
    x = full["L0_attn_a_x"]
    d["x_in"] = np.ascontiguousarray(x[2 * core:2 * core + 2].reshape(T, D))
    d["wqk"] = full["L0_attn_a_Wqk"]
    d["wv_a"] = full["L0_attn_a_Wv"]
    d["wn"] = full["L0_attn_a_Wn"]
    d["wo_a"] = full["L0_attn_a_Wo"]
    d["wq_b"] = np.ascontiguousarray(full["L0_attn_b_Wq"].transpose(1, 0, 2).reshape(D, D))
    d["wk_b"] = np.ascontiguousarray(full["L0_attn_b_Wk"].transpose(1, 0, 2).reshape(D, D))
    d["wv_b"] = np.ascontiguousarray(full["L0_attn_b_Wv"].transpose(1, 0, 2).reshape(D, D))
    d["wo_b"] = full["L0_attn_b_Wo"]
    d["w1"] = full["L0_ffn_w1"]
    d["w2"] = full["L0_ffn_w2"]
    d["bqk_pp"] = np.ascontiguousarray(full["L0_attn_a_bqk"].reshape(16, 128).T)
    d["bn_pp"] = np.ascontiguousarray(full["L0_attn_a_bn"].reshape(16, 1))
    d["bq_b_pp"] = np.ascontiguousarray(full["L0_attn_b_bq"].reshape(1024).reshape(8, 128).T)
    d["bk_b_pp"] = np.ascontiguousarray(full["L0_attn_b_bk"].reshape(1024).reshape(8, 128).T)
    d["b1_pp"] = np.ascontiguousarray(full["L0_ffn_b1"].reshape(32, 128).T)
    rows = np.zeros((128, 2048), np.float32)
    rows[0, 0:1024] = full["L0_attn_a_bv"]
    rows[0, 1024:2048] = full["L0_attn_a_bo"]
    rows[64, 0:1024] = full["L0_attn_b_bv"].reshape(1024)
    rows[64, 1024:2048] = full["L0_attn_b_bo"]
    d["rows_c"] = rows
    d["b2_row_f32"] = np.ascontiguousarray(full["L0_ffn_b2"].reshape(1, 1024))
    d["ident_c"] = np.eye(128, dtype=np.float32)
    jj = np.arange(896)[None, :]
    pp = np.arange(128)[:, None]
    d["cs_c"] = (jj >= pp + 384).astype(np.float32)
    esel = np.zeros((16, 1024), np.float32)
    for c in range(8):
        for pi in range(128):
            esel[2 * c + pi // 64, c * 128 + pi] = 1.0
    d["esel_c"] = esel
    om = np.zeros((128, 128), np.float32)
    om[64, 0:64] = 1.0
    om[32, 64:128] = 1.0
    d["omask_c"] = om
    cv = np.zeros((128, 8), np.float32)
    cv[:, 0] = 1e-5
    cv[:, 1] = -8.0
    d["cvals"] = cv
    return d


_NC_CACHE = {}


def kernel(**inputs):
    chain = ("attn_a", "ffn", "attn_b", "ffn")
    if chain not in _NC_CACHE:
        _NC_CACHE[chain] = build_nc(chain)
    nc = _NC_CACHE[chain]
    full = {k: np.asarray(val) for k, val in inputs.items()}
    in_maps = [_host_inputs(c, full) for c in range(8)]
    res = run_bass_kernel_spmd(nc, in_maps, core_ids=list(range(8)))
    out = np.concatenate([r["y"].reshape(2, S, D) for r in res.results], axis=0)
    return np.ascontiguousarray(out.astype(np.float32))


# version 15 (final)
# speedup vs baseline: 1.0072x; 1.0072x over previous
import numpy as np
import concourse.bass as bass
import concourse.mybir as mybir
from concourse.bass_utils import run_bass_kernel_spmd

F32 = mybir.dt.float32
BF16 = mybir.dt.bfloat16
I8 = mybir.dt.int8
AF = mybir.ActivationFunctionType
ALU = mybir.AluOpType
AX = mybir.AxisListType

SEQ, S, D, HID = 2, 2048, 1024, 4096
T = SEQ * S
GPS = S // 512

ENGS = ("sync", "tensor", "vector", "scalar", "gpsimd")
ENGMAP = {"q": "sync", "m": "tensor", "v": "vector", "s": "scalar", "g": "gpsimd"}


class Sem:
    def __init__(self, h):
        self.h = h
        self.n = 0


class P:
    def __init__(self, nc):
        self.nc = nc
        self.streams = {e: [] for e in ENGS}
        self.sem = {}
        for name in ("q", "g", "m", "v", "s"):
            cm = nc.semaphore(name)
            self.sem[name] = Sem(cm.__enter__())

    def wait(self, eng, semname, val):
        if val is None or val <= 0:
            return
        h = self.sem[semname].h
        self.streams[ENGMAP[eng]].append(lambda e, h=h, v=val: e.wait_ge(h, v))

    def op(self, eng, fn, semname=None, k=1):
        if semname is None:
            self.streams[ENGMAP[eng]].append(fn)
            return None
        sm = self.sem[semname]
        sm.n += k
        self.streams[ENGMAP[eng]].append(lambda e, fn=fn, h=sm.h, k=k: fn(e).then_inc(h, k))
        return sm.n

    def vchain(self, fn):
        n = self.op("v", fn, "v")
        self.wait("v", "v", n)
        return n

    def dma(self, eng, out, in_, semname):
        sm = self.sem[semname]
        sm.n += 16
        self.streams[ENGMAP[eng]].append(
            lambda e, o=out, i=in_, h=sm.h: e.dma_start(out=o, in_=i).then_inc(h, 16))
        return sm.n

    def run(self, blk):
        for eng in ENGS:
            stream = self.streams[eng]

            def body(e, stream=stream):
                for fn in stream:
                    fn(e)
            getattr(blk, eng)(body)


def build_nc(chain=("attn_a", "ffn", "attn_b", "ffn"), dbg=False):
    nc = bass.Bass("TRN2", target_bir_lowering=False)
    dram = {}

    def din(name, shape, dtype=F32):
        dram[name] = nc.dram_tensor(name, list(shape), dtype, kind="ExternalInput").ap()
        return dram[name]

    din("x_in", (T, D))
    y_out = nc.dram_tensor("y", [T, D], F32, kind="ExternalOutput").ap()

    for nm, sh in [("wqk", (D, 2048)), ("wv_a", (D, D)), ("wn", (D, 16)),
                   ("wo_a", (D, D)), ("wq_b", (D, D)), ("wk_b", (D, D)),
                   ("wv_b", (D, D)), ("wo_b", (D, D)), ("w1", (HID, D)),
                   ("w2", (D, HID)),
                   ("bqk_pp", (128, 16)), ("bn_pp", (16, 1)),
                   ("bq_b_pp", (128, 8)), ("bk_b_pp", (128, 8)),
                   ("b1_pp", (128, 32)), ("b2_row_f32", (1, 1024)),
                   ("rows_c", (128, 2048)),
                   ("ident_c", (128, 128)), ("cs_c", (128, 896)),
                   ("esel_c", (16, 1024)), ("omask_c", (128, 128)),
                   ("cvals", (128, 8))]:
        din(nm, sh)

    xsA = nc.dram_tensor("xsA", [T, D], F32, kind="Internal").ap()
    xsB = nc.dram_tensor("xsB", [T, D], F32, kind="Internal").ap()
    w1t_s = nc.dram_tensor("w1t_s", [D, HID], BF16, kind="Internal").ap()
    w2t_s = nc.dram_tensor("w2t_s", [HID, D], BF16, kind="Internal").ap()
    wot_s = nc.dram_tensor("wot_s", [D, D], BF16, kind="Internal").ap()

    ABC = 89216
    AFW = nc.alloc_sbuf_tensor([128, ABC], BF16)
    AFN = nc.alloc_sbuf_tensor([128, 5376], F32)
    H8t = nc.alloc_sbuf_tensor([128, 1024], I8)
    H8b = nc.alloc_sbuf_tensor([128, 1024], I8)
    WNs = nc.alloc_sbuf_tensor([128, 128], BF16)
    IDENT = nc.alloc_sbuf_tensor([128, 128], BF16)
    CSs = nc.alloc_sbuf_tensor([128, 896], BF16)
    ONES = nc.alloc_sbuf_tensor([128, 128], BF16)
    OMASK = nc.alloc_sbuf_tensor([128, 128], BF16)
    CV = nc.alloc_sbuf_tensor([128, 8], F32)
    BQK = nc.alloc_sbuf_tensor([128, 16], F32)
    BNn = nc.alloc_sbuf_tensor([16, 2], F32)
    BQB = nc.alloc_sbuf_tensor([128, 8], F32)
    BKB = nc.alloc_sbuf_tensor([128, 8], F32)
    B1Q = nc.alloc_sbuf_tensor([128, 64], F32)
    B1I = nc.alloc_sbuf_tensor([128, 32], I8)
    RWS = nc.alloc_sbuf_tensor([128, 3072], BF16)
    STAT = nc.alloc_sbuf_tensor([128, 32], F32)

    def abv(c0, c1, p0=0, p1=128):
        return AFW[p0:p1, c0:c1]

    def afv(c0, c1, p0=0, p1=128):
        return AFN[p0:p1, c0:c1]

    XT = [afv(0, 1024), afv(1024, 2048)]
    XR = [afv(2048, 3072), afv(3072, 4096)]
    XO = [afv(4096, 4608), afv(4608, 5120)]
    WFv = [afv(2048, 3072), afv(3072, 4096)]
    B2F = afv(4096, 5120, 0, 1)

    XN = [abv(0, 1024), abv(1024, 2048)]
    XNT0 = 2048
    SQD = abv(6144, 7168)

    WQS = abv(72704, 73728)
    ASM0 = 7168
    WSTG0 = 73728

    A_WQK, A_WV, A_WO = 7168, 23552, 31744
    A_QK, A_VA, A_VAO = 39936, 48128, 52224
    A_KTOK, A_EA, A_SSB = 56320, 58368, 60416
    A_CT, A_ENBC, A_ESEL, A_ENT = 61440, 65536, 67584, 68608
    F_W1, F_W2, F_HS = 7168, 39936, 72704
    B_WQ, B_WK, B_WV, B_WOT = 7168, 15360, 23552, 31744
    B_KT, B_VE = 39936, 56320
    B_QT, B_EB, B_EBM = 76800, 80896, 82944
    B_CT, B_BCB, B_RC = 83968, 88064, 88576
    assert B_RC + 512 <= ABC

    PS01 = nc.psum_tensor("ps01", [128, 1024], F32).__enter__()
    PS23 = nc.psum_tensor("ps23", [128, 1024], F32).__enter__()
    PS45 = nc.psum_tensor("ps45", [128, 1024], F32).__enter__()
    PS6 = nc.psum_tensor("ps6", [128, 512], F32).__enter__()
    PST = nc.psum_tensor("pst", [128, 1024], BF16).__enter__()

    p = P(nc)
    q, g, m, v, s = "q", "g", "m", "v", "s"

    def r3(ap_, **kw):
        return ap_.rearrange("p (a b) -> p a b", **kw)

    def recip_bf(e, out, in_):
        with nc.allow_low_precision(reason="recip feeds bf16 matmul rhs"):
            return e.reciprocal(out=out, in_=in_)

    p.dma(g, IDENT[:, :], dram["ident_c"], g)
    p.dma(g, CSs[:, :], dram["cs_c"], g)
    p.dma(g, abv(A_ESEL, A_ESEL + 1024, 0, 16), dram["esel_c"], g)
    p.dma(g, OMASK[:, :], dram["omask_c"], g)
    p.dma(g, RWS[:, 0:2048], dram["rows_c"], g)
    p.dma(q, CV[:, :], dram["cvals"], q)
    p.dma(q, BQK[:, :], dram["bqk_pp"], q)
    p.dma(q, BNn[0:16, 0:1], dram["bn_pp"], q)
    p.dma(q, BQB[:, :], dram["bq_b_pp"], q)
    p.dma(q, BKB[:, :], dram["bk_b_pp"], q)
    bias_q = p.dma(q, B1Q[:, 0:32], dram["b1_pp"], q)
    consts_g = p.sem[g].n
    p.op(v, lambda e: e.memset(ONES[:, :], 1.0), v)
    ones_v = p.sem[v].n

    p.wait(m, g, consts_g)

    p.wait(v, q, bias_q)
    p.vchain(lambda e: e.tensor_single_scalar(out=BNn[0:16, 1:2], in_=BNn[0:16, 0:1],
                                              scalar=-1.0, op=ALU.mult))
    p.vchain(lambda e: e.tensor_single_scalar(out=B1I[:, :], in_=B1Q[:, 0:32],
                                              scalar=64.0, op=ALU.mult))
    p.vchain(lambda e: e.tensor_single_scalar(out=B1Q[:, 32:64], in_=B1I[:, :],
                                              scalar=0.25, op=ALU.mult))
    ldb2 = p.dma(q, B2F, dram["b2_row_f32"], q)
    p.wait(v, q, ldb2)
    p.vchain(lambda e: e.tensor_single_scalar(out=H8t[0:1, 0:1024], in_=B2F,
                                              scalar=64.0, op=ALU.mult))
    p.vchain(lambda e: e.tensor_single_scalar(out=RWS[0:1, 2048:3072],
                                              in_=H8t[0:1, 0:1024],
                                              scalar=16.0, op=ALU.mult))
    b2_ready = p.sem[v].n

    wf_free = [0, 0]
    h8_free = [0]
    wqs_free = [0]

    def emit_wprep(src_dram, n_row_tiles, row_cols, dst_scratch, dst_rowlen, quantize,
                   pcount=[0]):
        nblk = row_cols // 128
        p.wait(s, g, consts_g)
        p.wait(q, v, b2_ready)
        for it in range(n_row_tiles):
            npieces = row_cols // 1024
            for pc in range(npieces):
                par = pcount[0] % 2
                pcount[0] += 1
                p.wait(q, s, wf_free[par])
                ld = p.dma(q, WFv[par],
                           src_dram[it * 128:(it + 1) * 128, pc * 1024:(pc + 1) * 1024], q)
                p.wait(s, q, ld)
                if quantize:
                    p.wait(s, v, h8_free[0])
                    ss = p.op(s, lambda e, par=par: e.activation(
                        out=H8t[:, :], in_=WFv[par], func=AF.Identity, scale=64.0,
                        bias=CV[:, 2:3]), s)
                    wf_free[par] = ss
                    p.wait(v, s, ss)
                    cp = p.op(v, lambda e: e.tensor_copy(WQS, H8t[:, :]), v)
                    h8_free[0] = cp
                    p.wait(m, v, cp)
                else:
                    p.wait(s, m, wqs_free[0])
                    ss = p.op(s, lambda e, par=par: e.activation(
                        out=WQS, in_=WFv[par], func=AF.Identity, scale=1.0,
                        bias=CV[:, 2:3]), s)
                    wf_free[par] = ss
                    p.wait(m, s, ss)
                for b in range(8):
                    p.op(m, lambda e, b=b: e.transpose(
                        PST[:, b * 128:(b + 1) * 128],
                        WQS[:, b * 128:(b + 1) * 128], IDENT[:, :]),
                        m if b == 7 else None)
                mv = p.sem[m].n
                wqs_free[0] = mv
                p.wait(v, m, mv)
                dst = AFW[:, ASM0:ASM0 + nblk * n_row_tiles * 128].rearrange(
                    "p (kc it j) -> p kc it j", kc=nblk, j=128)[:, pc * 8:pc * 8 + 8, it, :]
                ev = p.op(v, lambda e, dst=dst: e.tensor_copy(dst, PST[:, :]), v)
                p.wait(m, v, ev)
        evall = p.sem[v].n
        p.wait(q, v, evall)
        rowlen = n_row_tiles * 128
        last = 0
        for kc in range(nblk):
            last = p.dma(q, dst_scratch[kc * 128:(kc + 1) * 128, :],
                         abv(ASM0 + kc * rowlen, ASM0 + (kc + 1) * rowlen), q)
        p.wait(v, q, last)
        return p.sem[q].n

    if "ffn" in chain:
        emit_wprep(dram["w1"], 32, 1024, w1t_s, HID, True)
        emit_wprep(dram["w2"], 8, 4096, w2t_s, D, True)
    if "attn_b" in chain:
        emit_wprep(dram["wo_b"], 8, 1024, wot_s, D, False)

    aw_done = 0
    if "attn_a" in chain:
        p.wait(g, v, p.sem[v].n)
        p.wait(g, q, p.sem[q].n)
        for kc in range(8):
            p.dma(g, abv(A_WQK + kc * 2048, A_WQK + (kc + 1) * 2048),
                  dram["wqk"][kc * 128:(kc + 1) * 128, :], g)
            p.dma(g, abv(A_WV + kc * 1024, A_WV + (kc + 1) * 1024),
                  dram["wv_a"][kc * 128:(kc + 1) * 128, :], g)
            p.dma(g, abv(A_WO + kc * 1024, A_WO + (kc + 1) * 1024),
                  dram["wo_a"][kc * 128:(kc + 1) * 128, :], g)
            p.dma(g, WNs[:, kc * 16:(kc + 1) * 16],
                  dram["wn"][kc * 128:(kc + 1) * 128, :], g)
        aw_done = p.sem[g].n

    prep_g = p.sem[g].n
    prep_v = p.sem[v].n

    store_marks = {}

    def x_tile(src, t_idx):
        return src[t_idx * 128:(t_idx + 1) * 128, :]

    xt_free = [0, 0]
    xnt_guard = [0]
    xn_guard = [0, 0]
    xr_free = [0, 0]
    xo_free = [0, 0]

    xt_pre = {}

    def prefetch_x(src, stage_idx, gl, prev_stage):
        for tdx in (gl * 4, gl * 4 + 1):
            par = tdx % 2
            if prev_stage is not None:
                p.wait(q, g, store_marks.get((prev_stage, gl), 0))
            p.wait(q, v, xt_free[par])
            xt_pre[(stage_idx, tdx)] = p.dma(q, XT[par], x_tile(src, tdx), q)

    def ln_group(src, gl, quant, prev_stage, stage_idx):
        for hg in range(2):
            tiles = [gl * 4 + 2 * hg, gl * 4 + 2 * hg + 1]
            mz = p.op(v, lambda e: e.memset(STAT[:, 2:4], 0.0), v)
            stats_s = None
            for i, tdx in enumerate(tiles):
                par = (2 * hg + i) % 2
                ld = xt_pre.pop((stage_idx, tdx), None)
                if ld is None:
                    if prev_stage is not None:
                        p.wait(q, g, store_marks.get((prev_stage, gl), 0))
                    p.wait(q, v, xt_free[par])
                    ld = p.dma(q, XT[par], x_tile(src, tdx), q)
                p.wait(v, q, ld)
                p.op(v, lambda e, par=par, i=i: e.reduce_sum(
                    out=STAT[:, i:i + 1], in_=XT[par], axis=AX.X), v)
                p.wait(s, q, ld)
                p.wait(s, v, mz)
                stats_s = p.op(s, lambda e, par=par, i=i: e.activation(
                    out=SQD, in_=XT[par], func=AF.Square, bias=CV[:, 2:3],
                    accum_out=STAT[:, 2 + i:3 + i]), s)
            p.wait(v, s, stats_s)
            p.vchain(lambda e: e.tensor_single_scalar(out=STAT[:, 4:6], in_=STAT[:, 0:2],
                                                      scalar=1.0 / D, op=ALU.mult))
            p.op(v, lambda e: e.tensor_tensor(out=STAT[:, 6:8], in0=STAT[:, 4:6],
                                              in1=STAT[:, 4:6], op=ALU.mult))
            p.vchain(lambda e: e.tensor_single_scalar(out=STAT[:, 2:4], in_=STAT[:, 2:4],
                                                      scalar=1.0 / D, op=ALU.mult))
            p.vchain(lambda e: e.tensor_tensor(out=STAT[:, 8:10], in0=STAT[:, 2:4],
                                               in1=STAT[:, 6:8], op=ALU.subtract))
            p.vchain(lambda e: e.tensor_single_scalar(out=STAT[:, 10:12], in_=STAT[:, 8:10],
                                                      scalar=1e-5, op=ALU.add))
            vr = p.op(v, lambda e: e.reciprocal(out=STAT[:, 6:8], in_=STAT[:, 10:12]), v)
            p.wait(s, v, vr)
            sr = p.op(s, lambda e: e.activation(out=STAT[:, 8:10], in_=STAT[:, 6:8],
                                                func=AF.Sqrt, bias=CV[:, 2:3]), s)
            p.wait(v, s, sr)
            if quant:
                p.vchain(lambda e: e.tensor_single_scalar(out=STAT[:, 12:14],
                                                          in_=STAT[:, 8:10],
                                                          scalar=16.0, op=ALU.mult))
                p.vchain(lambda e: e.tensor_tensor(out=STAT[:, 14:16], in0=STAT[:, 4:6],
                                                   in1=STAT[:, 12:14], op=ALU.mult))
                acol, bcol = 12, 14
            else:
                p.vchain(lambda e: e.tensor_tensor(out=STAT[:, 12:14], in0=STAT[:, 4:6],
                                                   in1=STAT[:, 8:10], op=ALU.mult))
                acol, bcol = 8, 12
            for i in range(2):
                ti = 2 * hg + i
                par = ti % 2
                p.wait(v, m, xn_guard[par])
                if quant:
                    p.op(v, lambda e, par=par, i=i: e.tensor_scalar(
                        out=H8t[:, :], in0=XT[par],
                        scalar1=STAT[:, acol + i:acol + i + 1],
                        scalar2=STAT[:, bcol + i:bcol + i + 1],
                        op0=ALU.mult, op1=ALU.subtract))
                    ap_done = p.op(v, lambda e, par=par: e.tensor_copy(
                        XN[par], H8t[:, :]), v)
                else:
                    ap_done = p.op(v, lambda e, par=par, i=i: e.tensor_scalar(
                        out=XN[par], in0=XT[par],
                        scalar1=STAT[:, acol + i:acol + i + 1],
                        scalar2=STAT[:, bcol + i:bcol + i + 1],
                        op0=ALU.mult, op1=ALU.subtract), v)
                xt_free[par] = ap_done
                p.wait(m, v, ap_done)
                for b in range(8):
                    p.op(m, lambda e, b=b, par=par: e.transpose(
                        PST[:, b * 128:(b + 1) * 128],
                        XN[par][:, b * 128:(b + 1) * 128], IDENT[:, :]),
                        m if b == 7 else None)
                tm = p.sem[m].n
                xn_guard[par] = tm
                p.wait(v, m, tm)
                if ti == 0:
                    p.wait(v, m, xnt_guard[0])
                dst = AFW[:, XNT0:XNT0 + 4096].rearrange(
                    "p (kc tt b) -> p kc tt b", kc=8, tt=4)[:, :, ti, :]
                ev = p.op(v, lambda e, dst=dst: e.tensor_copy(dst, PST[:, :]), v)
                p.wait(m, v, ev)
        return p.sem[v].n

    def xnt(kc, cols=512, off=0):
        return abv(XNT0 + kc * 512 + off, XNT0 + kc * 512 + off + cols)

    ffn_n = [0]

    def ffn(src, dst, stage_idx, prev_stage):
        first = ffn_n[0] == 0
        ffn_n[0] += 1
        prefetch_x(src, stage_idx, 0, prev_stage)
        p.wait(q, m, p.sem[m].n)
        p.wait(q, v, p.sem[v].n)
        p.wait(q, g, p.sem[g].n)
        p.wait(q, q, p.sem[q].n)
        w_ready = 0
        for kc in range(8):
            w_ready = p.dma(q, abv(F_W1 + kc * 4096, F_W1 + (kc + 1) * 4096),
                            w1t_s[kc * 128:(kc + 1) * 128, :], q)
        for hc in range(32):
            w_ready = p.dma(q, abv(F_W2 + hc * 1024, F_W2 + (hc + 1) * 1024),
                            w2t_s[hc * 128:(hc + 1) * 128, :], q)
        hs_guard = [0]
        p2_ev = [0, 0]
        for gl in range(T // 512):
            ln_group(src, gl, True, prev_stage, stage_idx)
            xnt_ready = p.sem[v].n
            p.wait(m, v, xnt_ready)
            p.wait(m, q, w_ready)
            if first and gl == 0:
                p.wait(m, v, b2_ready)
            banks = [(PS01, 0), (PS01, 512), (PS23, 0), (PS23, 512)]
            ev_s = {}
            ev_v = {}
            for hc in range(32):
                bk, co = banks[hc % 4]
                p.wait(m, s, ev_s.get(hc - 4, 0))
                for kc in range(8):
                    p.op(m, lambda e, bk=bk, co=co, kc=kc, hc=hc: e.matmul(
                        bk[:, co:co + 512],
                        abv(F_W1 + kc * 4096 + hc * 128, F_W1 + kc * 4096 + hc * 128 + 128),
                        xnt(kc), start=(kc == 0), stop=(kc == 7)),
                        m if kc == 7 else None)
                mm = p.sem[m].n
                p.wait(s, m, mm)
                p.wait(s, v, ev_v.get(hc - 2, 0))
                sv = p.op(s, lambda e, bk=bk, co=co, hc=hc: e.activation(
                    out=H8b[:, (hc % 2) * 512:(hc % 2) * 512 + 512],
                    in_=bk[:, co:co + 512], func=AF.Identity, scale=1.0 / 64,
                    bias=B1Q[:, 32 + hc:33 + hc]), s)
                ev_s[hc] = sv
                p.wait(v, s, sv)
                if hc == 0:
                    p.wait(v, m, hs_guard[0])
                ev_v[hc] = p.op(v, lambda e, hc=hc: e.tensor_single_scalar(
                    out=abv(F_HS + hc * 512, F_HS + (hc + 1) * 512),
                    in_=H8b[:, (hc % 2) * 512:(hc % 2) * 512 + 512],
                    scalar=0.0, op=ALU.max), v)
            hs_ready = p.sem[v].n
            p.wait(m, v, hs_ready)
            xnt_guard[0] = p.sem[m].n
            for tt in range(4):
                t_idx = gl * 4 + tt
                par = tt % 2
                p.wait(q, v, xr_free[par])
                ldr = p.dma(q, XR[par], x_tile(src, t_idx), q)
                for half in range(2):
                    co2 = half * 512
                    p.wait(m, v, p2_ev[half])
                    for hc in range(32):
                        p.op(m, lambda e, co2=co2, hc=hc, tt=tt, half=half: e.matmul(
                            PS45[:, co2:co2 + 512],
                            abv(F_HS + hc * 512 + tt * 128, F_HS + hc * 512 + tt * 128 + 128),
                            abv(F_W2 + hc * 1024 + half * 512, F_W2 + hc * 1024 + half * 512 + 512),
                            start=(hc == 0), stop=False), None)
                    mm = p.op(m, lambda e, co2=co2, half=half: e.matmul(
                        PS45[:, co2:co2 + 512], ONES[0:1, 0:128],
                        RWS[0:1, 2048 + half * 512:2048 + half * 512 + 512],
                        start=False, stop=True), m)
                    p.wait(v, m, mm)
                    p.wait(v, q, ldr)
                    p.wait(v, g, xo_free[half])
                    ev = p.op(v, lambda e, co2=co2, par=par, half=half: e.scalar_tensor_tensor(
                        out=XO[half], in0=PS45[:, co2:co2 + 512], scalar=1.0 / 1024,
                        in1=XR[par][:, half * 512:half * 512 + 512],
                        op0=ALU.mult, op1=ALU.add), v)
                    p.wait(g, v, ev)
                    xo_free[half] = p.dma(
                        g, dst[t_idx * 128:(t_idx + 1) * 128, half * 512:(half + 1) * 512],
                        XO[half], g)
                    p2_ev[half] = ev
                xr_free[par] = p.sem[v].n
            hs_guard[0] = p.sem[m].n
            store_marks[(stage_idx, gl)] = p.sem[g].n

    def attn_a(src, dst, stage_idx, prev_stage):
        a_opj = [0, 0]
        vao_init = p.op(v, lambda e: e.memset(abv(A_VAO, A_VAO + 4096), 0.0), v)
        slab_guard = [0]
        for sq in range(SEQ):
            zdone = [0]
            for gi in range(GPS):
                gl = sq * GPS + gi
                ln_group(src, gl, False, prev_stage, stage_idx)
                if gi == 0:
                    p.wait(v, m, slab_guard[0])
                    zdone[0] = p.op(v, lambda e: e.memset(
                        abv(A_SSB, A_SSB + 1024), 0.0), v)
                xnt_ready = p.sem[v].n
                p.wait(m, v, xnt_ready)
                if sq == 0 and gi == 0:
                    p.wait(m, g, aw_done)
                    p.wait(m, g, prep_g)
                p.wait(m, v, a_opj[0])
                pj = [(PS01, 0), (PS01, 512), (PS6, 0)]
                qk_ev = {}
                for mc in range(16):
                    bk, co = pj[mc % 3]
                    p.wait(m, s, qk_ev.get(mc - 3, 0))
                    if mc == 0:
                        p.wait(s, m, slab_guard[0])
                    for kc in range(8):
                        p.op(m, lambda e, bk=bk, co=co, mc=mc, kc=kc: e.matmul(
                            bk[:, co:co + 512],
                            abv(A_WQK + kc * 2048 + mc * 128, A_WQK + kc * 2048 + mc * 128 + 128),
                            xnt(kc), start=(kc == 0), stop=(kc == 7)),
                            m if kc == 7 else None)
                    mm = p.sem[m].n
                    p.wait(s, m, mm)
                    qk_ev[mc] = p.op(s, lambda e, bk=bk, co=co, mc=mc: e.activation(
                        out=abv(A_QK + mc * 512, A_QK + (mc + 1) * 512),
                        in_=bk[:, co:co + 512], func=AF.Identity, scale=1.0,
                        bias=BQK[:, mc:mc + 1]), s)
                p.wait(m, v, a_opj[1])
                for kc in range(8):
                    p.op(m, lambda e, kc=kc: e.matmul(
                        PS23[0:16, 0:512], WNs[:, kc * 16:(kc + 1) * 16],
                        xnt(kc), start=(kc == 0), stop=(kc == 7)),
                        m if kc == 7 else None)
                mm = p.sem[m].n
                p.wait(s, m, mm)
                en_done = p.op(s, lambda e, gi=gi: e.activation(
                    out=abv(A_ENT + gi * 512, A_ENT + (gi + 1) * 512, 0, 16),
                    in_=PS23[0:16, 0:512], func=AF.Exp, scale=-1.0,
                    bias=BNn[0:16, 1:2]), s)
                p.wait(m, s, en_done)
                p.wait(m, v, ones_v)
                p.wait(m, v, vao_init)
                va_ev = [0, 0]
                for tt in range(4):
                    for half in range(2):
                        co2 = 512 * ((tt * 2 + half) % 2)
                        p.wait(m, v, va_ev[co2 // 512])
                        for kc in range(8):
                            p.op(m, lambda e, co2=co2, tt=tt, kc=kc, half=half: e.matmul(
                                PS23[:, co2:co2 + 512], xnt(kc, 128, tt * 128),
                                abv(A_WV + kc * 1024 + half * 512, A_WV + kc * 1024 + half * 512 + 512),
                                start=(kc == 0), stop=False), None)
                        mm = p.op(m, lambda e, co2=co2, half=half: e.matmul(
                            PS23[:, co2:co2 + 512], ONES[0:1, 0:128],
                            RWS[0:1, half * 512:half * 512 + 512],
                            start=False, stop=True), m)
                        p.wait(v, m, mm)
                        if tt == 0 and half == 0:
                            p.wait(v, m, slab_guard[0])
                        p.op(v, lambda e, co2=co2, tt=tt, half=half: e.tensor_copy(
                            abv(A_VA + tt * 1024 + half * 512, A_VA + tt * 1024 + half * 512 + 512),
                            PS23[:, co2:co2 + 512]))
                        dst_odd = r3(abv(A_VAO + tt * 1024 + half * 512,
                                         A_VAO + tt * 1024 + half * 512 + 512), a=4)[:, :, 64:128]
                        ev = p.op(v, lambda e, co2=co2, dst_odd=dst_odd: e.tensor_copy(
                            dst_odd, r3(PS23[:, co2:co2 + 512], a=4)[:, :, 64:128]), v)
                        va_ev[co2 // 512] = ev
                p.wait(m, v, max(va_ev[0], va_ev[1]))
                xnt_guard[0] = p.sem[m].n
                p.wait(m, v, zdone[0])
                ktok_guard = [0, 0]
                for c in range(4):
                    ci = gi * 4 + c
                    kpar = c % 2
                    epar = c % 2
                    for fc in range(8):
                        p.op(m, lambda e, fc=fc, c=c: e.transpose(
                            PST[:, fc * 128:(fc + 1) * 128],
                            abv(A_QK + (8 + fc) * 512 + c * 128,
                                A_QK + (8 + fc) * 512 + c * 128 + 128),
                            IDENT[:, :]), m if fc == 7 else None)
                    tm = p.sem[m].n
                    p.wait(v, m, tm)
                    p.wait(v, m, ktok_guard[kpar])
                    ktok_ev = p.op(v, lambda e, kpar=kpar: e.tensor_copy(
                        abv(A_KTOK + kpar * 1024, A_KTOK + (kpar + 1) * 1024),
                        PST[:, :]), v)
                    p.wait(m, v, ktok_ev)
                    for half in range(2):
                        for ff in range(4):
                            fc = half * 4 + ff
                            p.op(m, lambda e, fc=fc, ff=ff, gi=gi, c=c: e.matmul(
                                PS6[:, ff * 128:(ff + 1) * 128],
                                abv(A_ESEL + fc * 128, A_ESEL + (fc + 1) * 128, 0, 16),
                                abv(A_ENT + gi * 512 + c * 128,
                                    A_ENT + gi * 512 + c * 128 + 128, 0, 16),
                                start=True, stop=True), m if ff == 3 else None)
                        mm = p.sem[m].n
                        p.wait(v, m, mm)
                        enb_ev = p.op(v, lambda e, epar=epar, half=half: e.tensor_copy(
                            abv(A_ENBC + epar * 1024 + half * 512,
                                A_ENBC + epar * 1024 + (half + 1) * 512),
                            PS6[:, 0:512]), v)
                        p.wait(m, v, enb_ev)
                    if ci > 0:
                        for bi in range(2):
                            p.op(v, lambda e, bi=bi: e.tensor_copy(
                                r3(abv(A_SSB + bi * 512, A_SSB + (bi + 1) * 512, 0, 64),
                                   a=4)[:, :, 0:64],
                                r3(PS45[0:64, bi * 512:(bi + 1) * 512], a=4)[:, :, 0:64]))
                            p.op(v, lambda e, bi=bi: e.tensor_copy(
                                r3(abv(A_SSB + bi * 512, A_SSB + (bi + 1) * 512, 64, 128),
                                   a=4)[:, :, 64:128],
                                r3(PS45[64:128, bi * 512:(bi + 1) * 512], a=4)[:, :, 64:128]), v)
                        ssb_ev = p.sem[v].n
                        p.wait(m, v, ssb_ev)
                    for half in range(2):
                        for ff in range(4):
                            fc = half * 4 + ff
                            kbase = A_QK + (8 + fc) * 512 + c * 128
                            qbase = A_QK + fc * 512 + c * 128
                            p.op(m, lambda e, kbase=kbase, qbase=qbase, ff=ff: e.matmul(
                                PS01[:, ff * 128:(ff + 1) * 128],
                                abv(kbase, kbase + 128, 0, 64),
                                abv(qbase, qbase + 128, 0, 64),
                                start=True, stop=True), None)
                            p.op(m, lambda e, kbase=kbase, qbase=qbase, ff=ff: e.matmul(
                                PS01[:, 512 + ff * 128:512 + (ff + 1) * 128],
                                abv(kbase, kbase + 128, 64, 128),
                                abv(qbase, qbase + 128, 64, 128),
                                start=True, stop=True), m if ff == 3 else None)
                        mm = p.sem[m].n
                        p.wait(v, m, mm)
                        mrep = bass.AP(CSs, 384, [[896, 128], [0, 4], [1, 128]])
                        p.op(v, lambda e, half=half, mrep=mrep: e.tensor_tensor(
                            out=r3(abv(A_EA + half * 1024, A_EA + half * 1024 + 512), a=4),
                            in0=r3(PS01[:, 0:512], a=4), in1=mrep, op=ALU.mult))
                        ea_ev = p.op(v, lambda e, half=half, mrep=mrep: e.tensor_tensor(
                            out=r3(abv(A_EA + half * 1024 + 512, A_EA + half * 1024 + 1024), a=4),
                            in0=r3(PS01[:, 512:1024], a=4), in1=mrep, op=ALU.mult), v)
                        p.wait(m, v, ea_ev)
                    for fc in range(8):
                        cbk = PS23
                        co = (0 if fc < 4 else 512) + (fc % 4) * 128
                        qbase = A_QK + fc * 512 + c * 128
                        eabase = A_EA + (0 if fc < 4 else 1024) + (fc % 4) * 128
                        started = False
                        if ci > 0:
                            p.op(m, lambda e, cbk=cbk, co=co, fc=fc, qbase=qbase: e.matmul(
                                cbk[:, co:co + 128],
                                abv(A_SSB + fc * 128, A_SSB + (fc + 1) * 128),
                                abv(qbase, qbase + 128),
                                start=True, stop=False), None)
                            started = True
                        p.op(m, lambda e, cbk=cbk, co=co, fc=fc, c=c, eabase=eabase,
                             st=(not started): e.matmul(
                            cbk[:, co:co + 128],
                            abv(A_VAO + c * 1024 + fc * 128, A_VAO + c * 1024 + (fc + 1) * 128),
                            abv(eabase + 512, eabase + 512 + 128),
                            start=st, stop=False), None)
                        p.op(m, lambda e, cbk=cbk, co=co, fc=fc, c=c, eabase=eabase: e.matmul(
                            cbk[0:64, co:co + 128],
                            abv(A_VA + c * 1024 + fc * 128, A_VA + c * 1024 + fc * 128 + 64),
                            abv(eabase, eabase + 128),
                            start=False, stop=True), None)
                        sco = (0 if fc < 4 else 512) + (fc % 4) * 128
                        p.op(m, lambda e, sco=sco, fc=fc, c=c, kpar=kpar, ci=ci: e.matmul(
                            PS45[:, sco:sco + 128],
                            abv(A_KTOK + kpar * 1024 + fc * 128, A_KTOK + kpar * 1024 + (fc + 1) * 128),
                            abv(A_VA + c * 1024 + fc * 128, A_VA + c * 1024 + (fc + 1) * 128),
                            start=(ci == 0 and fc % 4 == 0), stop=(ci == S // 128 - 1)),
                            m if fc == 7 else None)
                    mm = p.sem[m].n
                    ktok_guard[kpar] = mm
                    p.wait(v, m, mm)
                    for bi in range(2):
                        dst2 = AFW[:, A_CT:A_CT + 4096].rearrange(
                            "p (fc tt b) -> p fc tt b", fc=8, tt=4)[:, bi * 4:(bi + 1) * 4, c, :]
                        p.op(v, lambda e, dst2=dst2, epar=epar, bi=bi: e.tensor_tensor(
                            out=dst2, in0=r3(PS23[:, bi * 512:(bi + 1) * 512], a=4),
                            in1=r3(abv(A_ENBC + epar * 1024 + bi * 512,
                                       A_ENBC + epar * 1024 + (bi + 1) * 512), a=4),
                            op=ALU.mult), v)
                    ctx_ev = p.sem[v].n
                    p.wait(m, v, ctx_ev)
                for tt in range(4):
                    t_idx = gl * 4 + tt
                    par = tt % 2
                    p.wait(q, v, xr_free[par])
                    ldr = p.dma(q, XR[par], x_tile(src, t_idx), q)
                    for half in range(2):
                        cpar = (tt * 2 + half) % 2
                        ACC = PS6[:, 0:512] if cpar == 0 else PS23[:, 0:512]
                        p.wait(m, v, a_opj[cpar])
                        for fc in range(8):
                            p.op(m, lambda e, fc=fc, tt=tt, half=half, ACC=ACC: e.matmul(
                                ACC,
                                abv(A_CT + fc * 512 + tt * 128, A_CT + fc * 512 + tt * 128 + 128),
                                abv(A_WO + fc * 1024 + half * 512, A_WO + fc * 1024 + half * 512 + 512),
                                start=(fc == 0), stop=False), None)
                        mm = p.op(m, lambda e, half=half, ACC=ACC: e.matmul(
                            ACC, ONES[0:1, 0:128],
                            RWS[0:1, 1024 + half * 512:1024 + half * 512 + 512],
                            start=False, stop=True), m)
                        p.wait(v, m, mm)
                        p.wait(v, q, ldr)
                        p.wait(v, g, xo_free[half])
                        ev = p.op(v, lambda e, par=par, half=half, ACC=ACC: e.tensor_tensor(
                            out=XO[half], in0=XR[par][:, half * 512:half * 512 + 512],
                            in1=ACC, op=ALU.add), v)
                        p.wait(g, v, ev)
                        xo_free[half] = p.dma(
                            g, dst[t_idx * 128:(t_idx + 1) * 128,
                                   half * 512:(half + 1) * 512], XO[half], g)
                        a_opj[cpar] = ev
                    xr_free[par] = p.sem[v].n
                store_marks[(stage_idx, gl)] = p.sem[g].n
                slab_guard[0] = p.sem[m].n

    def attn_b(src, dst, stage_idx, prev_stage):
        eb_guard = [0, 0]
        ebm_guard = [0]
        exp_done = [0, 0]
        ctev = [0]
        evac_done = [0]
        bcb_done = [0]
        norm_pend = [None]
        p.wait(g, m, p.sem[m].n)
        p.wait(g, v, p.sem[v].n)
        wqkv_ready = 0
        for kc in range(8):
            p.dma(g, abv(B_WQ + kc * 1024, B_WQ + (kc + 1) * 1024),
                  dram["wq_b"][kc * 128:(kc + 1) * 128, :], g)
            p.dma(g, abv(B_WK + kc * 1024, B_WK + (kc + 1) * 1024),
                  dram["wk_b"][kc * 128:(kc + 1) * 128, :], g)
            wqkv_ready = p.dma(g, abv(B_WV + kc * 1024, B_WV + (kc + 1) * 1024),
                               dram["wv_b"][kc * 128:(kc + 1) * 128, :], g)
        wot_ready = 0
        prefetch_x(src, stage_idx, 0, prev_stage)
        p.wait(q, m, p.sem[m].n)
        p.wait(q, g, p.sem[g].n)
        p.wait(q, q, p.sem[q].n)
        for kc in range(8):
            wot_ready = p.dma(q, abv(B_WOT + kc * 1024, B_WOT + (kc + 1) * 1024),
                              wot_s[kc * 128:(kc + 1) * 128, :], q)
        p.wait(v, m, p.sem[m].n)
        p.op(v, lambda e: e.memset(abv(B_VE, B_VE + 20480), 0.0))
        ones_cols = AFW[:, B_VE:B_VE + 20480].rearrange(
            "p (n blk) -> p n blk", blk=160)[:, :, 64:65]
        ve_init = p.op(v, lambda e, oc=ones_cols: e.memset(oc, 1.0), v)

        slab_guard = [0]
        for sq in range(SEQ):
            for gi in range(GPS):
                gl = sq * GPS + gi
                qc = gi
                ln_group(src, gl, False, prev_stage, stage_idx)
                xnt_ready = p.sem[v].n
                p.wait(m, v, xnt_ready)
                p.wait(m, g, wqkv_ready)
                p.wait(m, v, ve_init)
                p.wait(s, m, slab_guard[0])
                p.wait(m, s, exp_done[1])
                pj_ev = [0, 0]
                for fc in range(8):
                    for which in ("q", "k"):
                        co2 = 0 if which == "q" else 512
                        wbase = (B_WQ if which == "q" else B_WK) + fc * 128
                        p.wait(m, s, pj_ev[co2 // 512])
                        for kc in range(8):
                            p.op(m, lambda e, co2=co2, wbase=wbase, kc=kc: e.matmul(
                                PS45[:, co2:co2 + 512],
                                abv(wbase + kc * 1024, wbase + kc * 1024 + 128),
                                xnt(kc), start=(kc == 0), stop=(kc == 7)),
                                m if kc == 7 else None)
                        mm = p.sem[m].n
                        p.wait(s, m, mm)
                        if which == "q":
                            dstq = abv(B_QT + fc * 512, B_QT + (fc + 1) * 512)
                            bias = BQB[:, fc:fc + 1]
                        else:
                            dstq = abv(B_KT + fc * 2048 + gi * 512,
                                       B_KT + fc * 2048 + (gi + 1) * 512)
                            bias = BKB[:, fc:fc + 1]
                        sv = p.op(s, lambda e, co2=co2, dstq=dstq, bias=bias: e.activation(
                            out=dstq, in_=PS45[:, co2:co2 + 512], func=AF.Identity,
                            scale=1.0, bias=bias), s)
                        pj_ev[co2 // 512] = sv
                p.wait(m, s, max(pj_ev[0], pj_ev[1]))
                vv_ev = [pj_ev[0], pj_ev[1]]
                vv_is_s = [True, True]
                for tt in range(4):
                    kt = gi * 4 + tt
                    for half in range(2):
                        co2 = 512 * ((tt * 2 + half) % 2)
                        if vv_is_s[co2 // 512]:
                            p.wait(m, s, vv_ev[co2 // 512])
                        else:
                            p.wait(m, v, vv_ev[co2 // 512])
                        for kc in range(8):
                            p.op(m, lambda e, co2=co2, tt=tt, kc=kc, half=half: e.matmul(
                                PS45[:, co2:co2 + 512], xnt(kc, 128, tt * 128),
                                abv(B_WV + kc * 1024 + half * 512, B_WV + kc * 1024 + half * 512 + 512),
                                start=(kc == 0), stop=False), None)
                        mm = p.op(m, lambda e, co2=co2, half=half: e.matmul(
                            PS45[:, co2:co2 + 512], ONES[64:65, 0:128],
                            RWS[64:65, half * 512:half * 512 + 512],
                            start=False, stop=True), m)
                        p.wait(v, m, mm)
                        if tt == 0 and half == 0:
                            p.wait(v, m, slab_guard[0])
                        base = B_VE + kt * 1280 + half * 640
                        dste = AFW[:, base:base + 640].rearrange(
                            "p (a blk) -> p a blk", blk=160)
                        srcp = PS45[:, co2:co2 + 512].rearrange("p (a h b) -> p a h b", a=4, h=2)
                        p.op(v, lambda e, dste=dste, srcp=srcp: e.tensor_copy(
                            dste[:, :, 0:64], srcp[:, :, 0, :]))
                        ev = p.op(v, lambda e, dste=dste, srcp=srcp: e.tensor_copy(
                            dste[:, :, 96:160], srcp[:, :, 1, :]), v)
                        vv_ev[co2 // 512] = ev
                        vv_is_s[co2 // 512] = False
                last_v = max(vv_ev[0], vv_ev[1])
                p.wait(m, v, last_v)
                xnt_guard[0] = p.sem[m].n
                nkt = 4 * qc + 4

                def emit_ctx(kt, sv, qc=qc):
                    diag = kt >= 4 * qc
                    ebpar = kt % 2
                    ebdst = abv(B_EB + ebpar * 1024, B_EB + (ebpar + 1) * 1024)
                    if diag:
                        c0 = 384 - (kt * 128 - qc * 512)
                        p.wait(v, s, sv)
                        p.wait(v, m, ebm_guard[0])
                        msk = bass.AP(CSs, c0, [[896, 128], [0, 2], [1, 512]])
                        mev = p.op(v, lambda e, ebdst=ebdst, msk=msk: e.tensor_tensor(
                            out=r3(abv(B_EBM, B_EBM + 1024), a=2),
                            in0=r3(ebdst, a=2), in1=msk, op=ALU.mult), v)
                        p.wait(m, v, mev)
                        esrc = abv(B_EBM, B_EBM + 1024)
                    else:
                        p.wait(m, s, sv)
                        esrc = ebdst
                    vbase = B_VE + kt * 1280 + fc * 160
                    p.op(m, lambda e, vbase=vbase, esrc=esrc, kt=kt: e.matmul(
                        PS23[:, 0:512], abv(vbase, vbase + 128), esrc[:, 0:512],
                        start=(kt == 0), stop=(kt == nkt - 1)), None)
                    mm2 = p.op(m, lambda e, vbase=vbase, esrc=esrc, kt=kt: e.matmul(
                        PS23[:, 512:1024], abv(vbase + 32, vbase + 160), esrc[:, 512:1024],
                        start=(kt == 0), stop=(kt == nkt - 1)), m)
                    eb_guard[kt % 2] = mm2
                    if diag:
                        ebm_guard[0] = mm2

                def norm_chain():
                    fc_p, rcc = norm_pend[0]
                    norm_pend[0] = None
                    slot = fc_p % 2
                    p.wait(m, v, rcc)
                    p.wait(m, v, bcb_done[0])
                    b1 = p.op(m, lambda e, slot=slot: e.matmul(
                        PS6[:, 0:512], OMASK[64:65, 0:128],
                        abv(slot * 1024, slot * 1024 + 512, 64, 65),
                        start=True, stop=False), m)
                    p.wait(m, m, b1)
                    bmm = p.op(m, lambda e, slot=slot: e.matmul(
                        PS6[:, 0:512], OMASK[32:33, 0:128],
                        abv(slot * 1024, slot * 1024 + 512, 32, 33),
                        start=False, stop=True), m)
                    p.wait(v, m, bmm)
                    bcb_done[0] = p.op(v, lambda e: e.tensor_copy(
                        abv(B_BCB, B_BCB + 512), PS6[:, 0:512]), v)
                    p.op(v, lambda e, fc_p=fc_p: e.tensor_tensor(
                        out=abv(B_CT + fc_p * 512, B_CT + (fc_p + 1) * 512, 0, 64),
                        in0=afv(4096, 4608, 0, 64),
                        in1=abv(B_BCB, B_BCB + 512, 0, 64), op=ALU.mult))
                    ctev[0] = p.op(v, lambda e, fc_p=fc_p: e.tensor_tensor(
                        out=abv(B_CT + fc_p * 512, B_CT + (fc_p + 1) * 512, 64, 128),
                        in0=afv(4096, 4608, 64, 128),
                        in1=abv(B_BCB, B_BCB + 512, 64, 128), op=ALU.mult), v)

                for fc in range(8):
                    pend = []
                    for kt in range(nkt):
                        spar = kt % 2
                        SCB = PS01 if spar == 0 else PS45
                        p.wait(m, s, exp_done[spar])
                        if kt == 0:
                            p.wait(m, v, evac_done[0])
                        if kt == nkt - 1 and norm_pend[0] is not None:
                            norm_chain()
                        kbase = B_KT + fc * 2048 + kt * 128
                        p.op(m, lambda e, kbase=kbase, fc=fc, SCB=SCB: e.matmul(
                            SCB[:, 0:512], abv(kbase, kbase + 128, 0, 64),
                            abv(B_QT + fc * 512, B_QT + (fc + 1) * 512, 0, 64),
                            start=True, stop=True), None)
                        mm = p.op(m, lambda e, kbase=kbase, fc=fc, SCB=SCB: e.matmul(
                            SCB[:, 512:1024], abv(kbase, kbase + 128, 64, 128),
                            abv(B_QT + fc * 512, B_QT + (fc + 1) * 512, 64, 128),
                            start=True, stop=True), m)
                        if len(pend) == 2:
                            emit_ctx(*pend.pop(0))
                        p.wait(s, m, mm)
                        p.wait(s, m, eb_guard[kt % 2])
                        ebdst = abv(B_EB + (kt % 2) * 1024, B_EB + (kt % 2 + 1) * 1024)
                        sv = p.op(s, lambda e, ebdst=ebdst, SCB=SCB: e.activation(
                            out=ebdst, in_=SCB[:, 0:1024], func=AF.Exp,
                            scale=0.125, bias=CV[:, 1:2]), s)
                        exp_done[spar] = sv
                        pend.append((kt, sv))
                    for pe in pend:
                        emit_ctx(*pe)
                    mmend = p.sem[m].n
                    p.wait(v, m, mmend)
                    slot = fc % 2
                    if fc == 0:
                        p.wait(v, g, max(xo_free[0], xo_free[1]))
                    p.op(v, lambda e: e.tensor_copy(
                        afv(4096, 4608, 0, 64), PS23[0:64, 0:512]))
                    p.op(v, lambda e: e.tensor_copy(
                        afv(4608, 5120, 64, 65), PS23[64:65, 0:512]))
                    p.op(v, lambda e: e.tensor_copy(
                        afv(4608, 5120, 32, 33), PS23[32:33, 512:1024]))
                    ev2 = p.op(v, lambda e: e.tensor_copy(
                        afv(4096, 4608, 64, 128), PS23[64:128, 512:1024]), v)
                    evac_done[0] = ev2
                    p.op(v, lambda e, slot=slot: recip_bf(
                        e, abv(slot * 1024, slot * 1024 + 512, 64, 65),
                        afv(4608, 5120, 64, 65)))
                    rc = p.op(v, lambda e, slot=slot: recip_bf(
                        e, abv(slot * 1024, slot * 1024 + 512, 32, 33),
                        afv(4608, 5120, 32, 33)), v)
                    norm_pend[0] = (fc, rc)
                p.wait(m, q, wot_ready)
                b_opj = [evac_done[0], bcb_done[0]]
                ct7_waited = [False]
                for tt in range(4):
                    t_idx = gl * 4 + tt
                    par = tt % 2
                    p.wait(q, v, xr_free[par])
                    ldr = p.dma(q, XR[par], x_tile(src, t_idx), q)
                    for half in range(2):
                        cpar = (tt * 2 + half) % 2
                        ACC = PS23[:, 0:512] if cpar == 0 else PS6[:, 0:512]
                        p.wait(m, v, b_opj[cpar])
                        for fc in range(8):
                            if fc == 7 and not ct7_waited[0]:
                                if norm_pend[0] is not None:
                                    norm_chain()
                                    b_opj[1] = max(b_opj[1], bcb_done[0])
                                p.wait(m, v, ctev[0])
                                ct7_waited[0] = True
                            p.op(m, lambda e, fc=fc, tt=tt, half=half, ACC=ACC: e.matmul(
                                ACC,
                                abv(B_CT + fc * 512 + tt * 128, B_CT + fc * 512 + tt * 128 + 128),
                                abv(B_WOT + fc * 1024 + half * 512, B_WOT + fc * 1024 + half * 512 + 512),
                                start=(fc == 0), stop=False), None)
                        mm = p.op(m, lambda e, half=half, ACC=ACC: e.matmul(
                            ACC, ONES[64:65, 0:128],
                            RWS[64:65, 1024 + half * 512:1024 + half * 512 + 512],
                            start=False, stop=True), m)
                        p.wait(v, m, mm)
                        p.wait(v, q, ldr)
                        p.wait(v, g, xo_free[half])
                        ev = p.op(v, lambda e, par=par, half=half, ACC=ACC: e.tensor_tensor(
                            out=XO[half], in0=XR[par][:, half * 512:half * 512 + 512],
                            in1=ACC, op=ALU.add), v)
                        p.wait(g, v, ev)
                        xo_free[half] = p.dma(
                            g, dst[t_idx * 128:(t_idx + 1) * 128,
                                   half * 512:(half + 1) * 512], XO[half], g)
                        b_opj[cpar] = ev
                    xr_free[par] = p.sem[v].n
                evac_done[0] = b_opj[0]
                bcb_done[0] = b_opj[1]
                store_marks[(stage_idx, gl)] = p.sem[g].n
                slab_guard[0] = p.sem[m].n

    cur = dram["x_in"]
    scratch = [xsA, xsB]
    for i, kind in enumerate(chain):
        dst = y_out if i == len(chain) - 1 else scratch[i % 2]
        prev = i - 1 if i > 0 else None
        if kind == "ffn":
            ffn(cur, dst, i, prev)
        elif kind == "attn_a":
            attn_a(cur, dst, i, prev)
        elif kind == "attn_b":
            attn_b(cur, dst, i, prev)
        cur = dst

    if dbg:
        d1 = nc.dram_tensor("dbg_xnt", [128, 4096], F32, kind="ExternalOutput").ap()
        d2 = nc.dram_tensor("dbg_w1t", [128, 4096], F32, kind="ExternalOutput").ap()
        d3 = nc.dram_tensor("dbg_hs", [128, 16384], F32, kind="ExternalOutput").ap()
        d4 = nc.dram_tensor("dbg_w2t", [128, 1024], F32, kind="ExternalOutput").ap()
        p.wait(v, m, p.sem[m].n)
        p.wait(v, g, p.sem[g].n)
        p.wait(v, q, p.sem[q].n)
        p.wait(v, s, p.sem[s].n)
        p.wait(q, g, p.sem[g].n)
        p.wait(q, v, p.sem[v].n)
        p.wait(q, q, p.sem[q].n)
        for blk_i in range(4):
            cp = p.op(v, lambda e, blk_i=blk_i: e.tensor_copy(
                XT[0], abv(XNT0 + blk_i * 1024, XNT0 + (blk_i + 1) * 1024)), v)
            p.wait(g, v, cp)
            wr = p.dma(g, d1[:, blk_i * 1024:(blk_i + 1) * 1024], XT[0], g)
            p.wait(v, g, wr)
        ld = p.dma(q, abv(WSTG0, WSTG0 + 4096), w1t_s[0:128, :], q)
        p.wait(v, q, ld)
        for blk_i in range(4):
            cp = p.op(v, lambda e, blk_i=blk_i: e.tensor_copy(
                XT[0], abv(WSTG0 + blk_i * 1024, WSTG0 + (blk_i + 1) * 1024)), v)
            p.wait(g, v, cp)
            wr = p.dma(g, d2[:, blk_i * 1024:(blk_i + 1) * 1024], XT[0], g)
            p.wait(v, g, wr)
        for blk_i in range(16):
            cp = p.op(v, lambda e, blk_i=blk_i: e.tensor_copy(
                XT[0], abv(F_HS + blk_i * 1024, F_HS + (blk_i + 1) * 1024)), v)
            p.wait(g, v, cp)
            wr = p.dma(g, d3[:, blk_i * 1024:(blk_i + 1) * 1024], XT[0], g)
            p.wait(v, g, wr)
        ld = p.dma(q, abv(WSTG0, WSTG0 + 1024), w2t_s[0:128, :], q)
        p.wait(v, q, ld)
        cp = p.op(v, lambda e: e.tensor_copy(XT[0], abv(WSTG0, WSTG0 + 1024)), v)
        p.wait(g, v, cp)
        p.dma(g, d4[:, 0:1024], XT[0], g)

    p.wait(g, g, p.sem[g].n)
    p.wait(g, q, p.sem[q].n)
    p.wait(g, v, p.sem[v].n)
    p.wait(g, s, p.sem[s].n)
    p.wait(g, m, p.sem[m].n)

    with nc.Block() as blk:
        p.run(blk)
    return nc


def _host_inputs(core, full):
    d = {}
    x = full["L0_attn_a_x"]
    d["x_in"] = np.ascontiguousarray(x[2 * core:2 * core + 2].reshape(T, D))
    d["wqk"] = full["L0_attn_a_Wqk"]
    d["wv_a"] = full["L0_attn_a_Wv"]
    d["wn"] = full["L0_attn_a_Wn"]
    d["wo_a"] = full["L0_attn_a_Wo"]
    d["wq_b"] = np.ascontiguousarray(full["L0_attn_b_Wq"].transpose(1, 0, 2).reshape(D, D))
    d["wk_b"] = np.ascontiguousarray(full["L0_attn_b_Wk"].transpose(1, 0, 2).reshape(D, D))
    d["wv_b"] = np.ascontiguousarray(full["L0_attn_b_Wv"].transpose(1, 0, 2).reshape(D, D))
    d["wo_b"] = full["L0_attn_b_Wo"]
    d["w1"] = full["L0_ffn_w1"]
    d["w2"] = full["L0_ffn_w2"]
    d["bqk_pp"] = np.ascontiguousarray(full["L0_attn_a_bqk"].reshape(16, 128).T)
    d["bn_pp"] = np.ascontiguousarray(full["L0_attn_a_bn"].reshape(16, 1))
    d["bq_b_pp"] = np.ascontiguousarray(full["L0_attn_b_bq"].reshape(1024).reshape(8, 128).T)
    d["bk_b_pp"] = np.ascontiguousarray(full["L0_attn_b_bk"].reshape(1024).reshape(8, 128).T)
    d["b1_pp"] = np.ascontiguousarray(full["L0_ffn_b1"].reshape(32, 128).T)
    rows = np.zeros((128, 2048), np.float32)
    rows[0, 0:1024] = full["L0_attn_a_bv"]
    rows[0, 1024:2048] = full["L0_attn_a_bo"]
    rows[64, 0:1024] = full["L0_attn_b_bv"].reshape(1024)
    rows[64, 1024:2048] = full["L0_attn_b_bo"]
    d["rows_c"] = rows
    d["b2_row_f32"] = np.ascontiguousarray(full["L0_ffn_b2"].reshape(1, 1024))
    d["ident_c"] = np.eye(128, dtype=np.float32)
    jj = np.arange(896)[None, :]
    pp = np.arange(128)[:, None]
    d["cs_c"] = (jj >= pp + 384).astype(np.float32)
    esel = np.zeros((16, 1024), np.float32)
    for c in range(8):
        for pi in range(128):
            esel[2 * c + pi // 64, c * 128 + pi] = 1.0
    d["esel_c"] = esel
    om = np.zeros((128, 128), np.float32)
    om[64, 0:64] = 1.0
    om[32, 64:128] = 1.0
    d["omask_c"] = om
    cv = np.zeros((128, 8), np.float32)
    cv[:, 0] = 1e-5
    cv[:, 1] = -8.0
    d["cvals"] = cv
    return d


_NC_CACHE = {}


def kernel(**inputs):
    chain = ("attn_a", "ffn", "attn_b", "ffn")
    if chain not in _NC_CACHE:
        _NC_CACHE[chain] = build_nc(chain)
    nc = _NC_CACHE[chain]
    full = {k: np.asarray(val) for k, val in inputs.items()}
    in_maps = [_host_inputs(c, full) for c in range(8)]
    res = run_bass_kernel_spmd(nc, in_maps, core_ids=list(range(8)))
    out = np.concatenate([r["y"].reshape(2, S, D) for r in res.results], axis=0)
    return np.ascontiguousarray(out.astype(np.float32))


# version 17
# speedup vs baseline: 1.0129x; 1.0057x over previous
import numpy as np
import concourse.bass as bass
import concourse.mybir as mybir
from concourse.bass_utils import run_bass_kernel_spmd

F32 = mybir.dt.float32
BF16 = mybir.dt.bfloat16
I8 = mybir.dt.int8
AF = mybir.ActivationFunctionType
ALU = mybir.AluOpType
AX = mybir.AxisListType

SEQ, S, D, HID = 2, 2048, 1024, 4096
T = SEQ * S
GPS = S // 512

ENGS = ("sync", "tensor", "vector", "scalar", "gpsimd")
ENGMAP = {"q": "sync", "m": "tensor", "v": "vector", "s": "scalar", "g": "gpsimd"}


class Sem:
    def __init__(self, h):
        self.h = h
        self.n = 0


class P:
    def __init__(self, nc):
        self.nc = nc
        self.streams = {e: [] for e in ENGS}
        self.sem = {}
        for name in ("q", "g", "m", "v", "s"):
            cm = nc.semaphore(name)
            self.sem[name] = Sem(cm.__enter__())

    def wait(self, eng, semname, val):
        if val is None or val <= 0:
            return
        h = self.sem[semname].h
        self.streams[ENGMAP[eng]].append(lambda e, h=h, v=val: e.wait_ge(h, v))

    def op(self, eng, fn, semname=None, k=1):
        if semname is None:
            self.streams[ENGMAP[eng]].append(fn)
            return None
        sm = self.sem[semname]
        sm.n += k
        self.streams[ENGMAP[eng]].append(lambda e, fn=fn, h=sm.h, k=k: fn(e).then_inc(h, k))
        return sm.n

    def vchain(self, fn):
        n = self.op("v", fn, "v")
        self.wait("v", "v", n)
        return n

    def dma(self, eng, out, in_, semname):
        sm = self.sem[semname]
        sm.n += 16
        self.streams[ENGMAP[eng]].append(
            lambda e, o=out, i=in_, h=sm.h: e.dma_start(out=o, in_=i).then_inc(h, 16))
        return sm.n

    def run(self, blk):
        for eng in ENGS:
            stream = self.streams[eng]

            def body(e, stream=stream):
                for fn in stream:
                    fn(e)
            getattr(blk, eng)(body)


def build_nc(chain=("attn_a", "ffn", "attn_b", "ffn"), dbg=False):
    nc = bass.Bass("TRN2", target_bir_lowering=False)
    dram = {}

    def din(name, shape, dtype=F32):
        dram[name] = nc.dram_tensor(name, list(shape), dtype, kind="ExternalInput").ap()
        return dram[name]

    din("x_in", (T, D))
    y_out = nc.dram_tensor("y", [T, D], F32, kind="ExternalOutput").ap()

    for nm, sh in [("wqk", (D, 2048)), ("wv_a", (D, D)), ("wn", (D, 16)),
                   ("wo_a", (D, D)), ("wq_b", (D, D)), ("wk_b", (D, D)),
                   ("wv_b", (D, D)), ("wo_b", (D, D)), ("w1", (HID, D)),
                   ("w2", (D, HID)),
                   ("bqk_pp", (128, 16)), ("bn_pp", (16, 1)),
                   ("bq_b_pp", (128, 8)), ("bk_b_pp", (128, 8)),
                   ("b1_pp", (128, 32)), ("b2_row_f32", (1, 1024)),
                   ("rows_c", (128, 2048)),
                   ("ident_c", (128, 128)), ("cs_c", (128, 896)),
                   ("esel_c", (16, 1024)), ("omask_c", (128, 128)),
                   ("cvals", (128, 8))]:
        din(nm, sh)

    xsA = nc.dram_tensor("xsA", [T, D], F32, kind="Internal").ap()
    xsB = nc.dram_tensor("xsB", [T, D], F32, kind="Internal").ap()
    w1t_s = nc.dram_tensor("w1t_s", [D, HID], BF16, kind="Internal").ap()
    w2t_s = nc.dram_tensor("w2t_s", [HID, D], BF16, kind="Internal").ap()
    wot_s = nc.dram_tensor("wot_s", [D, D], BF16, kind="Internal").ap()

    ABC = 89216
    AFW = nc.alloc_sbuf_tensor([128, ABC], BF16)
    AFN = nc.alloc_sbuf_tensor([128, 5376], F32)
    H8t = nc.alloc_sbuf_tensor([128, 1024], I8)
    H8b = nc.alloc_sbuf_tensor([128, 1024], I8)
    WNs = nc.alloc_sbuf_tensor([128, 128], BF16)
    IDENT = nc.alloc_sbuf_tensor([128, 128], BF16)
    CSs = nc.alloc_sbuf_tensor([128, 896], BF16)
    ONES = nc.alloc_sbuf_tensor([128, 128], BF16)
    OMASK = nc.alloc_sbuf_tensor([128, 128], BF16)
    CV = nc.alloc_sbuf_tensor([128, 8], F32)
    BQK = nc.alloc_sbuf_tensor([128, 16], F32)
    BNn = nc.alloc_sbuf_tensor([16, 2], F32)
    BQB = nc.alloc_sbuf_tensor([128, 8], F32)
    BKB = nc.alloc_sbuf_tensor([128, 8], F32)
    B1Q = nc.alloc_sbuf_tensor([128, 64], F32)
    B1I = nc.alloc_sbuf_tensor([128, 32], I8)
    RWS = nc.alloc_sbuf_tensor([128, 3072], BF16)
    STAT = nc.alloc_sbuf_tensor([128, 32], F32)

    def abv(c0, c1, p0=0, p1=128):
        return AFW[p0:p1, c0:c1]

    def afv(c0, c1, p0=0, p1=128):
        return AFN[p0:p1, c0:c1]

    XT = [afv(0, 1024), afv(1024, 2048)]
    XR = [afv(2048, 3072), afv(3072, 4096)]
    XO = [afv(4096, 4608), afv(4608, 5120)]
    WFv = [afv(2048, 3072), afv(3072, 4096)]
    B2F = afv(4096, 5120, 0, 1)

    XN = [abv(0, 1024), abv(1024, 2048)]
    XNT0 = 2048
    SQD = abv(6144, 7168)

    WQS = abv(72704, 73728)
    ASM0 = 7168
    WSTG0 = 73728

    A_WQK, A_WV, A_WO = 7168, 23552, 31744
    A_QK, A_VA, A_VAO = 39936, 48128, 52224
    A_KTOK, A_EA, A_SSB = 56320, 58368, 60416
    A_CT, A_ENBC, A_ESEL, A_ENT = 61440, 65536, 67584, 68608
    F_W1, F_W2, F_HS = 7168, 39936, 72704
    B_WQ, B_WK, B_WV, B_WOT = 7168, 15360, 23552, 31744
    B_KT, B_VE = 39936, 56320
    B_QT, B_EB, B_EBM = 76800, 80896, 82944
    B_CT, B_BCB, B_RC = 83968, 88064, 88576
    assert B_RC + 512 <= ABC

    PS01 = nc.psum_tensor("ps01", [128, 1024], F32).__enter__()
    PS23 = nc.psum_tensor("ps23", [128, 1024], F32).__enter__()
    PS45 = nc.psum_tensor("ps45", [128, 1024], F32).__enter__()
    PS6 = nc.psum_tensor("ps6", [128, 512], F32).__enter__()
    PST = nc.psum_tensor("pst", [128, 1024], BF16).__enter__()

    p = P(nc)
    q, g, m, v, s = "q", "g", "m", "v", "s"

    def r3(ap_, **kw):
        return ap_.rearrange("p (a b) -> p a b", **kw)

    def recip_bf(e, out, in_):
        with nc.allow_low_precision(reason="recip feeds bf16 matmul rhs"):
            return e.reciprocal(out=out, in_=in_)

    p.dma(g, IDENT[:, :], dram["ident_c"], g)
    p.dma(g, CSs[:, :], dram["cs_c"], g)
    p.dma(g, abv(A_ESEL, A_ESEL + 1024, 0, 16), dram["esel_c"], g)
    p.dma(g, OMASK[:, :], dram["omask_c"], g)
    p.dma(g, RWS[:, 0:2048], dram["rows_c"], g)
    p.dma(q, CV[:, :], dram["cvals"], q)
    p.dma(q, BQK[:, :], dram["bqk_pp"], q)
    p.dma(q, BNn[0:16, 0:1], dram["bn_pp"], q)
    p.dma(q, BQB[:, :], dram["bq_b_pp"], q)
    p.dma(q, BKB[:, :], dram["bk_b_pp"], q)
    bias_q = p.dma(q, B1Q[:, 0:32], dram["b1_pp"], q)
    consts_g = p.sem[g].n
    p.op(v, lambda e: e.memset(ONES[:, :], 1.0), v)
    ones_v = p.sem[v].n

    p.wait(m, g, consts_g)

    p.wait(v, q, bias_q)
    p.vchain(lambda e: e.tensor_single_scalar(out=BNn[0:16, 1:2], in_=BNn[0:16, 0:1],
                                              scalar=-1.0, op=ALU.mult))
    p.vchain(lambda e: e.tensor_single_scalar(out=B1I[:, :], in_=B1Q[:, 0:32],
                                              scalar=64.0, op=ALU.mult))
    p.vchain(lambda e: e.tensor_single_scalar(out=B1Q[:, 32:64], in_=B1I[:, :],
                                              scalar=0.25, op=ALU.mult))
    ldb2 = p.dma(q, B2F, dram["b2_row_f32"], q)
    p.wait(v, q, ldb2)
    p.vchain(lambda e: e.tensor_single_scalar(out=H8t[0:1, 0:1024], in_=B2F,
                                              scalar=64.0, op=ALU.mult))
    p.vchain(lambda e: e.tensor_single_scalar(out=RWS[0:1, 2048:3072],
                                              in_=H8t[0:1, 0:1024],
                                              scalar=16.0, op=ALU.mult))
    b2_ready = p.sem[v].n

    wf_free = [0, 0]
    h8_free = [0]
    wqs_free = [0]

    def emit_wprep(src_dram, n_row_tiles, row_cols, dst_scratch, dst_rowlen, quantize,
                   pcount=[0]):
        nblk = row_cols // 128
        p.wait(s, g, consts_g)
        p.wait(q, v, b2_ready)
        for it in range(n_row_tiles):
            npieces = row_cols // 1024
            for pc in range(npieces):
                par = pcount[0] % 2
                pcount[0] += 1
                p.wait(q, s, wf_free[par])
                ld = p.dma(q, WFv[par],
                           src_dram[it * 128:(it + 1) * 128, pc * 1024:(pc + 1) * 1024], q)
                p.wait(s, q, ld)
                if quantize:
                    p.wait(s, v, h8_free[0])
                    ss = p.op(s, lambda e, par=par: e.activation(
                        out=H8t[:, :], in_=WFv[par], func=AF.Identity, scale=64.0,
                        bias=CV[:, 2:3]), s)
                    wf_free[par] = ss
                    p.wait(v, s, ss)
                    cp = p.op(v, lambda e: e.tensor_copy(WQS, H8t[:, :]), v)
                    h8_free[0] = cp
                    p.wait(m, v, cp)
                else:
                    p.wait(s, m, wqs_free[0])
                    ss = p.op(s, lambda e, par=par: e.activation(
                        out=WQS, in_=WFv[par], func=AF.Identity, scale=1.0,
                        bias=CV[:, 2:3]), s)
                    wf_free[par] = ss
                    p.wait(m, s, ss)
                for b in range(8):
                    p.op(m, lambda e, b=b: e.transpose(
                        PST[:, b * 128:(b + 1) * 128],
                        WQS[:, b * 128:(b + 1) * 128], IDENT[:, :]),
                        m if b == 7 else None)
                mv = p.sem[m].n
                wqs_free[0] = mv
                p.wait(v, m, mv)
                dst = AFW[:, ASM0:ASM0 + nblk * n_row_tiles * 128].rearrange(
                    "p (kc it j) -> p kc it j", kc=nblk, j=128)[:, pc * 8:pc * 8 + 8, it, :]
                ev = p.op(v, lambda e, dst=dst: e.tensor_copy(dst, PST[:, :]), v)
                p.wait(m, v, ev)
        evall = p.sem[v].n
        p.wait(q, v, evall)
        rowlen = n_row_tiles * 128
        last = 0
        for kc in range(nblk):
            last = p.dma(q, dst_scratch[kc * 128:(kc + 1) * 128, :],
                         abv(ASM0 + kc * rowlen, ASM0 + (kc + 1) * rowlen), q)
        p.wait(v, q, last)
        return p.sem[q].n

    if "ffn" in chain:
        emit_wprep(dram["w1"], 32, 1024, w1t_s, HID, True)
        emit_wprep(dram["w2"], 8, 4096, w2t_s, D, True)
    if "attn_b" in chain:
        emit_wprep(dram["wo_b"], 8, 1024, wot_s, D, False)

    aw_done = 0
    if "attn_a" in chain:
        p.wait(g, v, p.sem[v].n)
        p.wait(g, q, p.sem[q].n)
        for kc in range(8):
            p.dma(g, abv(A_WQK + kc * 2048, A_WQK + (kc + 1) * 2048),
                  dram["wqk"][kc * 128:(kc + 1) * 128, :], g)
            p.dma(g, abv(A_WV + kc * 1024, A_WV + (kc + 1) * 1024),
                  dram["wv_a"][kc * 128:(kc + 1) * 128, :], g)
            p.dma(g, abv(A_WO + kc * 1024, A_WO + (kc + 1) * 1024),
                  dram["wo_a"][kc * 128:(kc + 1) * 128, :], g)
            p.dma(g, WNs[:, kc * 16:(kc + 1) * 16],
                  dram["wn"][kc * 128:(kc + 1) * 128, :], g)
        aw_done = p.sem[g].n

    prep_g = p.sem[g].n
    prep_v = p.sem[v].n

    store_marks = {}

    def x_tile(src, t_idx):
        return src[t_idx * 128:(t_idx + 1) * 128, :]

    xt_free = [0, 0]
    xnt_guard = [0]
    xn_guard = [0, 0]
    xr_free = [0, 0]
    xo_free = [0, 0]

    xt_pre = {}

    def prefetch_x(src, stage_idx, gl, prev_stage):
        for tdx in (gl * 4, gl * 4 + 1):
            par = tdx % 2
            if prev_stage is not None:
                p.wait(q, g, store_marks.get((prev_stage, gl), 0))
            p.wait(q, v, xt_free[par])
            xt_pre[(stage_idx, tdx)] = p.dma(q, XT[par], x_tile(src, tdx), q)

    def ln_group(src, gl, quant, prev_stage, stage_idx):
        for hg in range(2):
            tiles = [gl * 4 + 2 * hg, gl * 4 + 2 * hg + 1]
            mz = p.op(v, lambda e: e.memset(STAT[:, 2:4], 0.0), v)
            stats_s = None
            for i, tdx in enumerate(tiles):
                par = (2 * hg + i) % 2
                ld = xt_pre.pop((stage_idx, tdx), None)
                if ld is None:
                    if prev_stage is not None:
                        p.wait(q, g, store_marks.get((prev_stage, gl), 0))
                    p.wait(q, v, xt_free[par])
                    ld = p.dma(q, XT[par], x_tile(src, tdx), q)
                p.wait(v, q, ld)
                p.op(v, lambda e, par=par, i=i: e.reduce_sum(
                    out=STAT[:, i:i + 1], in_=XT[par], axis=AX.X), v)
                p.wait(s, q, ld)
                p.wait(s, v, mz)
                stats_s = p.op(s, lambda e, par=par, i=i: e.activation(
                    out=SQD, in_=XT[par], func=AF.Square, bias=CV[:, 2:3],
                    accum_out=STAT[:, 2 + i:3 + i]), s)
            p.wait(v, s, stats_s)
            p.vchain(lambda e: e.tensor_single_scalar(out=STAT[:, 4:6], in_=STAT[:, 0:2],
                                                      scalar=1.0 / D, op=ALU.mult))
            p.op(v, lambda e: e.tensor_tensor(out=STAT[:, 6:8], in0=STAT[:, 4:6],
                                              in1=STAT[:, 4:6], op=ALU.mult))
            p.vchain(lambda e: e.tensor_single_scalar(out=STAT[:, 2:4], in_=STAT[:, 2:4],
                                                      scalar=1.0 / D, op=ALU.mult))
            p.vchain(lambda e: e.tensor_tensor(out=STAT[:, 8:10], in0=STAT[:, 2:4],
                                               in1=STAT[:, 6:8], op=ALU.subtract))
            p.vchain(lambda e: e.tensor_single_scalar(out=STAT[:, 10:12], in_=STAT[:, 8:10],
                                                      scalar=1e-5, op=ALU.add))
            vr = p.op(v, lambda e: e.reciprocal(out=STAT[:, 6:8], in_=STAT[:, 10:12]), v)
            p.wait(s, v, vr)
            sr = p.op(s, lambda e: e.activation(out=STAT[:, 8:10], in_=STAT[:, 6:8],
                                                func=AF.Sqrt, bias=CV[:, 2:3]), s)
            p.wait(v, s, sr)
            if quant:
                p.vchain(lambda e: e.tensor_single_scalar(out=STAT[:, 12:14],
                                                          in_=STAT[:, 8:10],
                                                          scalar=16.0, op=ALU.mult))
                p.vchain(lambda e: e.tensor_tensor(out=STAT[:, 14:16], in0=STAT[:, 4:6],
                                                   in1=STAT[:, 12:14], op=ALU.mult))
                acol, bcol = 12, 14
            else:
                p.vchain(lambda e: e.tensor_tensor(out=STAT[:, 12:14], in0=STAT[:, 4:6],
                                                   in1=STAT[:, 8:10], op=ALU.mult))
                acol, bcol = 8, 12
            for i in range(2):
                ti = 2 * hg + i
                par = ti % 2
                p.wait(v, m, xn_guard[par])
                if quant:
                    p.op(v, lambda e, par=par, i=i: e.tensor_scalar(
                        out=H8t[:, :], in0=XT[par],
                        scalar1=STAT[:, acol + i:acol + i + 1],
                        scalar2=STAT[:, bcol + i:bcol + i + 1],
                        op0=ALU.mult, op1=ALU.subtract))
                    ap_done = p.op(v, lambda e, par=par: e.tensor_copy(
                        XN[par], H8t[:, :]), v)
                else:
                    ap_done = p.op(v, lambda e, par=par, i=i: e.tensor_scalar(
                        out=XN[par], in0=XT[par],
                        scalar1=STAT[:, acol + i:acol + i + 1],
                        scalar2=STAT[:, bcol + i:bcol + i + 1],
                        op0=ALU.mult, op1=ALU.subtract), v)
                xt_free[par] = ap_done
                p.wait(m, v, ap_done)
                for b in range(8):
                    p.op(m, lambda e, b=b, par=par: e.transpose(
                        PST[:, b * 128:(b + 1) * 128],
                        XN[par][:, b * 128:(b + 1) * 128], IDENT[:, :]),
                        m if b == 7 else None)
                tm = p.sem[m].n
                xn_guard[par] = tm
                p.wait(v, m, tm)
                if ti == 0:
                    p.wait(v, m, xnt_guard[0])
                dst = AFW[:, XNT0:XNT0 + 4096].rearrange(
                    "p (kc tt b) -> p kc tt b", kc=8, tt=4)[:, :, ti, :]
                ev = p.op(v, lambda e, dst=dst: e.tensor_copy(dst, PST[:, :]), v)
                p.wait(m, v, ev)
        return p.sem[v].n

    def xnt(kc, cols=512, off=0):
        return abv(XNT0 + kc * 512 + off, XNT0 + kc * 512 + off + cols)

    ffn_n = [0]

    def ffn(src, dst, stage_idx, prev_stage):
        first = ffn_n[0] == 0
        ffn_n[0] += 1
        prefetch_x(src, stage_idx, 0, prev_stage)
        p.wait(q, m, p.sem[m].n)
        p.wait(q, v, p.sem[v].n)
        p.wait(q, g, p.sem[g].n)
        p.wait(q, q, p.sem[q].n)
        w_ready = 0
        for kc in range(8):
            w_ready = p.dma(q, abv(F_W1 + kc * 4096, F_W1 + (kc + 1) * 4096),
                            w1t_s[kc * 128:(kc + 1) * 128, :], q)
        for hc in range(32):
            w_ready = p.dma(q, abv(F_W2 + hc * 1024, F_W2 + (hc + 1) * 1024),
                            w2t_s[hc * 128:(hc + 1) * 128, :], q)
        hs_guard = [0]
        p2_ev = [0, 0]
        for gl in range(T // 512):
            ln_group(src, gl, True, prev_stage, stage_idx)
            xnt_ready = p.sem[v].n
            p.wait(m, v, xnt_ready)
            p.wait(m, q, w_ready)
            if first and gl == 0:
                p.wait(m, v, b2_ready)
            banks = [(PS01, 0), (PS01, 512), (PS23, 0), (PS23, 512)]
            ev_s = {}
            ev_v = {}
            for hc in range(32):
                bk, co = banks[hc % 4]
                p.wait(m, s, ev_s.get(hc - 4, 0))
                for kc in range(8):
                    p.op(m, lambda e, bk=bk, co=co, kc=kc, hc=hc: e.matmul(
                        bk[:, co:co + 512],
                        abv(F_W1 + kc * 4096 + hc * 128, F_W1 + kc * 4096 + hc * 128 + 128),
                        xnt(kc), start=(kc == 0), stop=(kc == 7)),
                        m if kc == 7 else None)
                mm = p.sem[m].n
                p.wait(s, m, mm)
                p.wait(s, v, ev_v.get(hc - 2, 0))
                sv = p.op(s, lambda e, bk=bk, co=co, hc=hc: e.activation(
                    out=H8b[:, (hc % 2) * 512:(hc % 2) * 512 + 512],
                    in_=bk[:, co:co + 512], func=AF.Identity, scale=1.0 / 64,
                    bias=B1Q[:, 32 + hc:33 + hc]), s)
                ev_s[hc] = sv
                p.wait(v, s, sv)
                if hc == 0:
                    p.wait(v, m, hs_guard[0])
                ev_v[hc] = p.op(v, lambda e, hc=hc: e.tensor_single_scalar(
                    out=abv(F_HS + hc * 512, F_HS + (hc + 1) * 512),
                    in_=H8b[:, (hc % 2) * 512:(hc % 2) * 512 + 512],
                    scalar=0.0, op=ALU.max), v)
            hs_ready = p.sem[v].n
            p.wait(m, v, hs_ready)
            xnt_guard[0] = p.sem[m].n
            for tt in range(4):
                t_idx = gl * 4 + tt
                par = tt % 2
                p.wait(q, v, xr_free[par])
                ldr = p.dma(q, XR[par], x_tile(src, t_idx), q)
                for half in range(2):
                    co2 = half * 512
                    p.wait(m, v, p2_ev[half])
                    for hc in range(32):
                        p.op(m, lambda e, co2=co2, hc=hc, tt=tt, half=half: e.matmul(
                            PS45[:, co2:co2 + 512],
                            abv(F_HS + hc * 512 + tt * 128, F_HS + hc * 512 + tt * 128 + 128),
                            abv(F_W2 + hc * 1024 + half * 512, F_W2 + hc * 1024 + half * 512 + 512),
                            start=(hc == 0), stop=False), None)
                    mm = p.op(m, lambda e, co2=co2, half=half: e.matmul(
                        PS45[:, co2:co2 + 512], ONES[0:1, 0:128],
                        RWS[0:1, 2048 + half * 512:2048 + half * 512 + 512],
                        start=False, stop=True), m)
                    p.wait(v, m, mm)
                    p.wait(v, q, ldr)
                    p.wait(v, g, xo_free[half])
                    ev = p.op(v, lambda e, co2=co2, par=par, half=half: e.scalar_tensor_tensor(
                        out=XO[half], in0=PS45[:, co2:co2 + 512], scalar=1.0 / 1024,
                        in1=XR[par][:, half * 512:half * 512 + 512],
                        op0=ALU.mult, op1=ALU.add), v)
                    p.wait(g, v, ev)
                    xo_free[half] = p.dma(
                        g, dst[t_idx * 128:(t_idx + 1) * 128, half * 512:(half + 1) * 512],
                        XO[half], g)
                    p2_ev[half] = ev
                xr_free[par] = p.sem[v].n
            hs_guard[0] = p.sem[m].n
            store_marks[(stage_idx, gl)] = p.sem[g].n

    def attn_a(src, dst, stage_idx, prev_stage):
        a_opj = [0, 0]
        vao_init = p.op(v, lambda e: e.memset(abv(A_VAO, A_VAO + 4096), 0.0), v)
        slab_guard = [0]
        for sq in range(SEQ):
            zdone = [0]
            for gi in range(GPS):
                gl = sq * GPS + gi
                ln_group(src, gl, False, prev_stage, stage_idx)
                if gi == 0:
                    p.wait(v, m, slab_guard[0])
                    zdone[0] = p.op(v, lambda e: e.memset(
                        abv(A_SSB, A_SSB + 1024), 0.0), v)
                xnt_ready = p.sem[v].n
                p.wait(m, v, xnt_ready)
                if sq == 0 and gi == 0:
                    p.wait(m, g, aw_done)
                    p.wait(m, g, prep_g)
                p.wait(m, v, a_opj[0])
                pj = [(PS01, 0), (PS01, 512), (PS6, 0)]
                qk_ev = {}
                for mc in range(16):
                    bk, co = pj[mc % 3]
                    p.wait(m, s, qk_ev.get(mc - 3, 0))
                    if mc == 0:
                        p.wait(s, m, slab_guard[0])
                    for kc in range(8):
                        p.op(m, lambda e, bk=bk, co=co, mc=mc, kc=kc: e.matmul(
                            bk[:, co:co + 512],
                            abv(A_WQK + kc * 2048 + mc * 128, A_WQK + kc * 2048 + mc * 128 + 128),
                            xnt(kc), start=(kc == 0), stop=(kc == 7)),
                            m if kc == 7 else None)
                    mm = p.sem[m].n
                    p.wait(s, m, mm)
                    qk_ev[mc] = p.op(s, lambda e, bk=bk, co=co, mc=mc: e.activation(
                        out=abv(A_QK + mc * 512, A_QK + (mc + 1) * 512),
                        in_=bk[:, co:co + 512], func=AF.Identity, scale=1.0,
                        bias=BQK[:, mc:mc + 1]), s)
                p.wait(m, v, a_opj[1])
                for kc in range(8):
                    p.op(m, lambda e, kc=kc: e.matmul(
                        PS23[0:16, 0:512], WNs[:, kc * 16:(kc + 1) * 16],
                        xnt(kc), start=(kc == 0), stop=(kc == 7)),
                        m if kc == 7 else None)
                mm = p.sem[m].n
                p.wait(s, m, mm)
                en_done = p.op(s, lambda e, gi=gi: e.activation(
                    out=abv(A_ENT + gi * 512, A_ENT + (gi + 1) * 512, 0, 16),
                    in_=PS23[0:16, 0:512], func=AF.Exp, scale=-1.0,
                    bias=BNn[0:16, 1:2]), s)
                p.wait(m, s, en_done)
                p.wait(m, v, ones_v)
                p.wait(m, v, vao_init)
                va_ev = [0, 0]
                for tt in range(4):
                    for half in range(2):
                        co2 = 512 * ((tt * 2 + half) % 2)
                        p.wait(m, v, va_ev[co2 // 512])
                        for kc in range(8):
                            p.op(m, lambda e, co2=co2, tt=tt, kc=kc, half=half: e.matmul(
                                PS23[:, co2:co2 + 512], xnt(kc, 128, tt * 128),
                                abv(A_WV + kc * 1024 + half * 512, A_WV + kc * 1024 + half * 512 + 512),
                                start=(kc == 0), stop=False), None)
                        mm = p.op(m, lambda e, co2=co2, half=half: e.matmul(
                            PS23[:, co2:co2 + 512], ONES[0:1, 0:128],
                            RWS[0:1, half * 512:half * 512 + 512],
                            start=False, stop=True), m)
                        p.wait(v, m, mm)
                        if tt == 0 and half == 0:
                            p.wait(v, m, slab_guard[0])
                        p.op(v, lambda e, co2=co2, tt=tt, half=half: e.tensor_copy(
                            abv(A_VA + tt * 1024 + half * 512, A_VA + tt * 1024 + half * 512 + 512),
                            PS23[:, co2:co2 + 512]))
                        dst_odd = r3(abv(A_VAO + tt * 1024 + half * 512,
                                         A_VAO + tt * 1024 + half * 512 + 512), a=4)[:, :, 64:128]
                        ev = p.op(v, lambda e, co2=co2, dst_odd=dst_odd: e.tensor_copy(
                            dst_odd, r3(PS23[:, co2:co2 + 512], a=4)[:, :, 64:128]), v)
                        va_ev[co2 // 512] = ev
                p.wait(m, v, max(va_ev[0], va_ev[1]))
                xnt_guard[0] = p.sem[m].n
                p.wait(m, v, zdone[0])
                ktok_guard = [0, 0]
                ctxg = [0]
                for c in range(4):
                    ci = gi * 4 + c
                    kpar = c % 2
                    epar = c % 2
                    for fc in range(8):
                        p.op(m, lambda e, fc=fc, c=c: e.transpose(
                            PST[:, fc * 128:(fc + 1) * 128],
                            abv(A_QK + (8 + fc) * 512 + c * 128,
                                A_QK + (8 + fc) * 512 + c * 128 + 128),
                            IDENT[:, :]), m if fc == 7 else None)
                    tm = p.sem[m].n
                    p.wait(v, m, tm)
                    p.wait(v, m, ktok_guard[kpar])
                    ktok_ev = p.op(v, lambda e, kpar=kpar: e.tensor_copy(
                        abv(A_KTOK + kpar * 1024, A_KTOK + (kpar + 1) * 1024),
                        PST[:, :]), v)
                    for half in range(2):
                        for ff in range(4):
                            fc = half * 4 + ff
                            p.op(m, lambda e, fc=fc, ff=ff, gi=gi, c=c: e.matmul(
                                PS6[:, ff * 128:(ff + 1) * 128],
                                abv(A_ESEL + fc * 128, A_ESEL + (fc + 1) * 128, 0, 16),
                                abv(A_ENT + gi * 512 + c * 128,
                                    A_ENT + gi * 512 + c * 128 + 128, 0, 16),
                                start=True, stop=True), m if ff == 3 else None)
                        mm = p.sem[m].n
                        p.wait(v, m, mm)
                        enb_ev = p.op(v, lambda e, epar=epar, half=half: e.tensor_copy(
                            abv(A_ENBC + epar * 1024 + half * 512,
                                A_ENBC + epar * 1024 + (half + 1) * 512),
                            PS6[:, 0:512]), v)
                        p.wait(m, v, enb_ev)
                    if ci > 0:
                        for bi in range(2):
                            p.op(v, lambda e, bi=bi: e.tensor_copy(
                                r3(abv(A_SSB + bi * 512, A_SSB + (bi + 1) * 512, 0, 64),
                                   a=4)[:, :, 0:64],
                                r3(PS45[0:64, bi * 512:(bi + 1) * 512], a=4)[:, :, 0:64]))
                            p.op(v, lambda e, bi=bi: e.tensor_copy(
                                r3(abv(A_SSB + bi * 512, A_SSB + (bi + 1) * 512, 64, 128),
                                   a=4)[:, :, 64:128],
                                r3(PS45[64:128, bi * 512:(bi + 1) * 512], a=4)[:, :, 64:128]), v)
                        ssb_ev = p.sem[v].n
                    for half in range(2):
                        for ff in range(4):
                            fc = half * 4 + ff
                            kbase = A_QK + (8 + fc) * 512 + c * 128
                            qbase = A_QK + fc * 512 + c * 128
                            p.op(m, lambda e, kbase=kbase, qbase=qbase, ff=ff: e.matmul(
                                PS01[:, ff * 128:(ff + 1) * 128],
                                abv(kbase, kbase + 128, 0, 64),
                                abv(qbase, qbase + 128, 0, 64),
                                start=True, stop=True), None)
                            p.op(m, lambda e, kbase=kbase, qbase=qbase, ff=ff: e.matmul(
                                PS01[:, 512 + ff * 128:512 + (ff + 1) * 128],
                                abv(kbase, kbase + 128, 64, 128),
                                abv(qbase, qbase + 128, 64, 128),
                                start=True, stop=True), m if ff == 3 else None)
                        mm = p.sem[m].n
                        p.wait(v, m, mm)
                        mrep = bass.AP(CSs, 384, [[896, 128], [0, 4], [1, 128]])
                        p.op(v, lambda e, half=half, mrep=mrep: e.tensor_tensor(
                            out=r3(abv(A_EA + half * 1024, A_EA + half * 1024 + 512), a=4),
                            in0=r3(PS01[:, 0:512], a=4), in1=mrep, op=ALU.mult))
                        ea_ev = p.op(v, lambda e, half=half, mrep=mrep: e.tensor_tensor(
                            out=r3(abv(A_EA + half * 1024 + 512, A_EA + half * 1024 + 1024), a=4),
                            in0=r3(PS01[:, 512:1024], a=4), in1=mrep, op=ALU.mult), v)
                        p.wait(m, v, ea_ev)
                    p.wait(m, v, ktok_ev)
                    if ci > 0:
                        p.wait(m, v, ssb_ev)
                    if c > 0:
                        p.wait(m, v, ctxg[0])
                    for fc in range(8):
                        cbk = PS23
                        co = (0 if fc < 4 else 512) + (fc % 4) * 128
                        qbase = A_QK + fc * 512 + c * 128
                        eabase = A_EA + (0 if fc < 4 else 1024) + (fc % 4) * 128
                        started = False
                        if ci > 0:
                            p.op(m, lambda e, cbk=cbk, co=co, fc=fc, qbase=qbase: e.matmul(
                                cbk[:, co:co + 128],
                                abv(A_SSB + fc * 128, A_SSB + (fc + 1) * 128),
                                abv(qbase, qbase + 128),
                                start=True, stop=False), None)
                            started = True
                        p.op(m, lambda e, cbk=cbk, co=co, fc=fc, c=c, eabase=eabase,
                             st=(not started): e.matmul(
                            cbk[:, co:co + 128],
                            abv(A_VAO + c * 1024 + fc * 128, A_VAO + c * 1024 + (fc + 1) * 128),
                            abv(eabase + 512, eabase + 512 + 128),
                            start=st, stop=False), None)
                        p.op(m, lambda e, cbk=cbk, co=co, fc=fc, c=c, eabase=eabase: e.matmul(
                            cbk[0:64, co:co + 128],
                            abv(A_VA + c * 1024 + fc * 128, A_VA + c * 1024 + fc * 128 + 64),
                            abv(eabase, eabase + 128),
                            start=False, stop=True), None)
                        sco = (0 if fc < 4 else 512) + (fc % 4) * 128
                        p.op(m, lambda e, sco=sco, fc=fc, c=c, kpar=kpar, ci=ci: e.matmul(
                            PS45[:, sco:sco + 128],
                            abv(A_KTOK + kpar * 1024 + fc * 128, A_KTOK + kpar * 1024 + (fc + 1) * 128),
                            abv(A_VA + c * 1024 + fc * 128, A_VA + c * 1024 + (fc + 1) * 128),
                            start=(ci == 0 and fc % 4 == 0), stop=(ci == S // 128 - 1)),
                            m if fc == 7 else None)
                    mm = p.sem[m].n
                    ktok_guard[kpar] = mm
                    p.wait(v, m, mm)
                    for bi in range(2):
                        dst2 = AFW[:, A_CT:A_CT + 4096].rearrange(
                            "p (fc tt b) -> p fc tt b", fc=8, tt=4)[:, bi * 4:(bi + 1) * 4, c, :]
                        p.op(v, lambda e, dst2=dst2, epar=epar, bi=bi: e.tensor_tensor(
                            out=dst2, in0=r3(PS23[:, bi * 512:(bi + 1) * 512], a=4),
                            in1=r3(abv(A_ENBC + epar * 1024 + bi * 512,
                                       A_ENBC + epar * 1024 + (bi + 1) * 512), a=4),
                            op=ALU.mult), v)
                    ctxg[0] = p.sem[v].n
                p.wait(m, v, ctxg[0])
                for tt in range(4):
                    t_idx = gl * 4 + tt
                    par = tt % 2
                    p.wait(q, v, xr_free[par])
                    ldr = p.dma(q, XR[par], x_tile(src, t_idx), q)
                    for half in range(2):
                        cpar = (tt * 2 + half) % 2
                        ACC = PS6[:, 0:512] if cpar == 0 else PS23[:, 0:512]
                        p.wait(m, v, a_opj[cpar])
                        for fc in range(8):
                            p.op(m, lambda e, fc=fc, tt=tt, half=half, ACC=ACC: e.matmul(
                                ACC,
                                abv(A_CT + fc * 512 + tt * 128, A_CT + fc * 512 + tt * 128 + 128),
                                abv(A_WO + fc * 1024 + half * 512, A_WO + fc * 1024 + half * 512 + 512),
                                start=(fc == 0), stop=False), None)
                        mm = p.op(m, lambda e, half=half, ACC=ACC: e.matmul(
                            ACC, ONES[0:1, 0:128],
                            RWS[0:1, 1024 + half * 512:1024 + half * 512 + 512],
                            start=False, stop=True), m)
                        p.wait(v, m, mm)
                        p.wait(v, q, ldr)
                        p.wait(v, g, xo_free[half])
                        ev = p.op(v, lambda e, par=par, half=half, ACC=ACC: e.tensor_tensor(
                            out=XO[half], in0=XR[par][:, half * 512:half * 512 + 512],
                            in1=ACC, op=ALU.add), v)
                        p.wait(g, v, ev)
                        xo_free[half] = p.dma(
                            g, dst[t_idx * 128:(t_idx + 1) * 128,
                                   half * 512:(half + 1) * 512], XO[half], g)
                        a_opj[cpar] = ev
                    xr_free[par] = p.sem[v].n
                store_marks[(stage_idx, gl)] = p.sem[g].n
                slab_guard[0] = p.sem[m].n

    def attn_b(src, dst, stage_idx, prev_stage):
        eb_guard = [0, 0]
        ebm_guard = [0]
        exp_done = [0, 0]
        ctev = [0]
        evac_done = [0]
        bcb_done = [0]
        norm_pend = [None]
        p.wait(g, m, p.sem[m].n)
        p.wait(g, v, p.sem[v].n)
        wqkv_ready = 0
        for kc in range(8):
            p.dma(g, abv(B_WQ + kc * 1024, B_WQ + (kc + 1) * 1024),
                  dram["wq_b"][kc * 128:(kc + 1) * 128, :], g)
            p.dma(g, abv(B_WK + kc * 1024, B_WK + (kc + 1) * 1024),
                  dram["wk_b"][kc * 128:(kc + 1) * 128, :], g)
            wqkv_ready = p.dma(g, abv(B_WV + kc * 1024, B_WV + (kc + 1) * 1024),
                               dram["wv_b"][kc * 128:(kc + 1) * 128, :], g)
        wot_ready = 0
        prefetch_x(src, stage_idx, 0, prev_stage)
        p.wait(q, m, p.sem[m].n)
        p.wait(q, g, p.sem[g].n)
        p.wait(q, q, p.sem[q].n)
        for kc in range(8):
            wot_ready = p.dma(q, abv(B_WOT + kc * 1024, B_WOT + (kc + 1) * 1024),
                              wot_s[kc * 128:(kc + 1) * 128, :], q)
        p.wait(v, m, p.sem[m].n)
        p.op(v, lambda e: e.memset(abv(B_VE, B_VE + 20480), 0.0))
        ones_cols = AFW[:, B_VE:B_VE + 20480].rearrange(
            "p (n blk) -> p n blk", blk=160)[:, :, 64:65]
        ve_init = p.op(v, lambda e, oc=ones_cols: e.memset(oc, 1.0), v)

        slab_guard = [0]
        for sq in range(SEQ):
            for gi in range(GPS):
                gl = sq * GPS + gi
                qc = gi
                ln_group(src, gl, False, prev_stage, stage_idx)
                xnt_ready = p.sem[v].n
                p.wait(m, v, xnt_ready)
                p.wait(m, g, wqkv_ready)
                p.wait(m, v, ve_init)
                p.wait(s, m, slab_guard[0])
                p.wait(m, s, exp_done[1])
                pj_ev = [0, 0]
                for fc in range(8):
                    for which in ("q", "k"):
                        co2 = 0 if which == "q" else 512
                        wbase = (B_WQ if which == "q" else B_WK) + fc * 128
                        p.wait(m, s, pj_ev[co2 // 512])
                        for kc in range(8):
                            p.op(m, lambda e, co2=co2, wbase=wbase, kc=kc: e.matmul(
                                PS45[:, co2:co2 + 512],
                                abv(wbase + kc * 1024, wbase + kc * 1024 + 128),
                                xnt(kc), start=(kc == 0), stop=(kc == 7)),
                                m if kc == 7 else None)
                        mm = p.sem[m].n
                        p.wait(s, m, mm)
                        if which == "q":
                            dstq = abv(B_QT + fc * 512, B_QT + (fc + 1) * 512)
                            bias = BQB[:, fc:fc + 1]
                        else:
                            dstq = abv(B_KT + fc * 2048 + gi * 512,
                                       B_KT + fc * 2048 + (gi + 1) * 512)
                            bias = BKB[:, fc:fc + 1]
                        sv = p.op(s, lambda e, co2=co2, dstq=dstq, bias=bias: e.activation(
                            out=dstq, in_=PS45[:, co2:co2 + 512], func=AF.Identity,
                            scale=1.0, bias=bias), s)
                        pj_ev[co2 // 512] = sv
                p.wait(m, s, max(pj_ev[0], pj_ev[1]))
                vv_ev = [pj_ev[0], pj_ev[1]]
                vv_is_s = [True, True]
                for tt in range(4):
                    kt = gi * 4 + tt
                    for half in range(2):
                        co2 = 512 * ((tt * 2 + half) % 2)
                        if vv_is_s[co2 // 512]:
                            p.wait(m, s, vv_ev[co2 // 512])
                        else:
                            p.wait(m, v, vv_ev[co2 // 512])
                        for kc in range(8):
                            p.op(m, lambda e, co2=co2, tt=tt, kc=kc, half=half: e.matmul(
                                PS45[:, co2:co2 + 512], xnt(kc, 128, tt * 128),
                                abv(B_WV + kc * 1024 + half * 512, B_WV + kc * 1024 + half * 512 + 512),
                                start=(kc == 0), stop=False), None)
                        mm = p.op(m, lambda e, co2=co2, half=half: e.matmul(
                            PS45[:, co2:co2 + 512], ONES[64:65, 0:128],
                            RWS[64:65, half * 512:half * 512 + 512],
                            start=False, stop=True), m)
                        p.wait(v, m, mm)
                        if tt == 0 and half == 0:
                            p.wait(v, m, slab_guard[0])
                        base = B_VE + kt * 1280 + half * 640
                        dste = AFW[:, base:base + 640].rearrange(
                            "p (a blk) -> p a blk", blk=160)
                        srcp = PS45[:, co2:co2 + 512].rearrange("p (a h b) -> p a h b", a=4, h=2)
                        p.op(v, lambda e, dste=dste, srcp=srcp: e.tensor_copy(
                            dste[:, :, 0:64], srcp[:, :, 0, :]))
                        ev = p.op(v, lambda e, dste=dste, srcp=srcp: e.tensor_copy(
                            dste[:, :, 96:160], srcp[:, :, 1, :]), v)
                        vv_ev[co2 // 512] = ev
                        vv_is_s[co2 // 512] = False
                last_v = max(vv_ev[0], vv_ev[1])
                p.wait(m, v, last_v)
                xnt_guard[0] = p.sem[m].n
                nkt = 4 * qc + 4

                def emit_ctx(kt, sv, qc=qc):
                    diag = kt >= 4 * qc
                    ebpar = kt % 2
                    ebdst = abv(B_EB + ebpar * 1024, B_EB + (ebpar + 1) * 1024)
                    if diag:
                        c0 = 384 - (kt * 128 - qc * 512)
                        p.wait(v, s, sv)
                        p.wait(v, m, ebm_guard[0])
                        msk = bass.AP(CSs, c0, [[896, 128], [0, 2], [1, 512]])
                        mev = p.op(v, lambda e, ebdst=ebdst, msk=msk: e.tensor_tensor(
                            out=r3(abv(B_EBM, B_EBM + 1024), a=2),
                            in0=r3(ebdst, a=2), in1=msk, op=ALU.mult), v)
                        p.wait(m, v, mev)
                        esrc = abv(B_EBM, B_EBM + 1024)
                    else:
                        p.wait(m, s, sv)
                        esrc = ebdst
                    vbase = B_VE + kt * 1280 + fc * 160
                    p.op(m, lambda e, vbase=vbase, esrc=esrc, kt=kt: e.matmul(
                        PS23[:, 0:512], abv(vbase, vbase + 128), esrc[:, 0:512],
                        start=(kt == 0), stop=(kt == nkt - 1)), None)
                    mm2 = p.op(m, lambda e, vbase=vbase, esrc=esrc, kt=kt: e.matmul(
                        PS23[:, 512:1024], abv(vbase + 32, vbase + 160), esrc[:, 512:1024],
                        start=(kt == 0), stop=(kt == nkt - 1)), m)
                    eb_guard[kt % 2] = mm2
                    if diag:
                        ebm_guard[0] = mm2

                def norm_chain():
                    fc_p, rcc = norm_pend[0]
                    norm_pend[0] = None
                    slot = fc_p % 2
                    p.wait(m, v, rcc)
                    p.wait(m, v, bcb_done[0])
                    b1 = p.op(m, lambda e, slot=slot: e.matmul(
                        PS6[:, 0:512], OMASK[64:65, 0:128],
                        abv(slot * 1024, slot * 1024 + 512, 64, 65),
                        start=True, stop=False), m)
                    p.wait(m, m, b1)
                    bmm = p.op(m, lambda e, slot=slot: e.matmul(
                        PS6[:, 0:512], OMASK[32:33, 0:128],
                        abv(slot * 1024, slot * 1024 + 512, 32, 33),
                        start=False, stop=True), m)
                    p.wait(v, m, bmm)
                    bcb_done[0] = p.op(v, lambda e: e.tensor_copy(
                        abv(B_BCB, B_BCB + 512), PS6[:, 0:512]), v)
                    p.op(v, lambda e, fc_p=fc_p: e.tensor_tensor(
                        out=abv(B_CT + fc_p * 512, B_CT + (fc_p + 1) * 512, 0, 64),
                        in0=afv(4096, 4608, 0, 64),
                        in1=abv(B_BCB, B_BCB + 512, 0, 64), op=ALU.mult))
                    ctev[0] = p.op(v, lambda e, fc_p=fc_p: e.tensor_tensor(
                        out=abv(B_CT + fc_p * 512, B_CT + (fc_p + 1) * 512, 64, 128),
                        in0=afv(4096, 4608, 64, 128),
                        in1=abv(B_BCB, B_BCB + 512, 64, 128), op=ALU.mult), v)

                for fc in range(8):
                    pend = []
                    for kt in range(nkt):
                        spar = kt % 2
                        SCB = PS01 if spar == 0 else PS45
                        p.wait(m, s, exp_done[spar])
                        if kt == 0:
                            p.wait(m, v, evac_done[0])
                        if kt == nkt - 1 and norm_pend[0] is not None:
                            norm_chain()
                        kbase = B_KT + fc * 2048 + kt * 128
                        p.op(m, lambda e, kbase=kbase, fc=fc, SCB=SCB: e.matmul(
                            SCB[:, 0:512], abv(kbase, kbase + 128, 0, 64),
                            abv(B_QT + fc * 512, B_QT + (fc + 1) * 512, 0, 64),
                            start=True, stop=True), None)
                        mm = p.op(m, lambda e, kbase=kbase, fc=fc, SCB=SCB: e.matmul(
                            SCB[:, 512:1024], abv(kbase, kbase + 128, 64, 128),
                            abv(B_QT + fc * 512, B_QT + (fc + 1) * 512, 64, 128),
                            start=True, stop=True), m)
                        if len(pend) == 2:
                            emit_ctx(*pend.pop(0))
                        p.wait(s, m, mm)
                        p.wait(s, m, eb_guard[kt % 2])
                        ebdst = abv(B_EB + (kt % 2) * 1024, B_EB + (kt % 2 + 1) * 1024)
                        sv = p.op(s, lambda e, ebdst=ebdst, SCB=SCB: e.activation(
                            out=ebdst, in_=SCB[:, 0:1024], func=AF.Exp,
                            scale=0.125, bias=CV[:, 1:2]), s)
                        exp_done[spar] = sv
                        pend.append((kt, sv))
                    for pe in pend:
                        emit_ctx(*pe)
                    mmend = p.sem[m].n
                    p.wait(v, m, mmend)
                    slot = fc % 2
                    if fc == 0:
                        p.wait(v, g, max(xo_free[0], xo_free[1]))
                    p.op(v, lambda e: e.tensor_copy(
                        afv(4096, 4608, 0, 64), PS23[0:64, 0:512]))
                    p.op(v, lambda e: e.tensor_copy(
                        afv(4608, 5120, 64, 65), PS23[64:65, 0:512]))
                    p.op(v, lambda e: e.tensor_copy(
                        afv(4608, 5120, 32, 33), PS23[32:33, 512:1024]))
                    ev2 = p.op(v, lambda e: e.tensor_copy(
                        afv(4096, 4608, 64, 128), PS23[64:128, 512:1024]), v)
                    evac_done[0] = ev2
                    p.op(v, lambda e, slot=slot: recip_bf(
                        e, abv(slot * 1024, slot * 1024 + 512, 64, 65),
                        afv(4608, 5120, 64, 65)))
                    rc = p.op(v, lambda e, slot=slot: recip_bf(
                        e, abv(slot * 1024, slot * 1024 + 512, 32, 33),
                        afv(4608, 5120, 32, 33)), v)
                    norm_pend[0] = (fc, rc)
                p.wait(m, q, wot_ready)
                b_opj = [evac_done[0], bcb_done[0]]
                ct7_waited = [False]
                for tt in range(4):
                    t_idx = gl * 4 + tt
                    par = tt % 2
                    p.wait(q, v, xr_free[par])
                    ldr = p.dma(q, XR[par], x_tile(src, t_idx), q)
                    for half in range(2):
                        cpar = (tt * 2 + half) % 2
                        ACC = PS23[:, 0:512] if cpar == 0 else PS6[:, 0:512]
                        p.wait(m, v, b_opj[cpar])
                        for fc in range(8):
                            if fc == 7 and not ct7_waited[0]:
                                if norm_pend[0] is not None:
                                    norm_chain()
                                    b_opj[1] = max(b_opj[1], bcb_done[0])
                                p.wait(m, v, ctev[0])
                                ct7_waited[0] = True
                            p.op(m, lambda e, fc=fc, tt=tt, half=half, ACC=ACC: e.matmul(
                                ACC,
                                abv(B_CT + fc * 512 + tt * 128, B_CT + fc * 512 + tt * 128 + 128),
                                abv(B_WOT + fc * 1024 + half * 512, B_WOT + fc * 1024 + half * 512 + 512),
                                start=(fc == 0), stop=False), None)
                        mm = p.op(m, lambda e, half=half, ACC=ACC: e.matmul(
                            ACC, ONES[64:65, 0:128],
                            RWS[64:65, 1024 + half * 512:1024 + half * 512 + 512],
                            start=False, stop=True), m)
                        p.wait(v, m, mm)
                        p.wait(v, q, ldr)
                        p.wait(v, g, xo_free[half])
                        ev = p.op(v, lambda e, par=par, half=half, ACC=ACC: e.tensor_tensor(
                            out=XO[half], in0=XR[par][:, half * 512:half * 512 + 512],
                            in1=ACC, op=ALU.add), v)
                        p.wait(g, v, ev)
                        xo_free[half] = p.dma(
                            g, dst[t_idx * 128:(t_idx + 1) * 128,
                                   half * 512:(half + 1) * 512], XO[half], g)
                        b_opj[cpar] = ev
                    xr_free[par] = p.sem[v].n
                evac_done[0] = b_opj[0]
                bcb_done[0] = b_opj[1]
                store_marks[(stage_idx, gl)] = p.sem[g].n
                slab_guard[0] = p.sem[m].n

    cur = dram["x_in"]
    scratch = [xsA, xsB]
    for i, kind in enumerate(chain):
        dst = y_out if i == len(chain) - 1 else scratch[i % 2]
        prev = i - 1 if i > 0 else None
        if kind == "ffn":
            ffn(cur, dst, i, prev)
        elif kind == "attn_a":
            attn_a(cur, dst, i, prev)
        elif kind == "attn_b":
            attn_b(cur, dst, i, prev)
        cur = dst

    if dbg:
        d1 = nc.dram_tensor("dbg_xnt", [128, 4096], F32, kind="ExternalOutput").ap()
        d2 = nc.dram_tensor("dbg_w1t", [128, 4096], F32, kind="ExternalOutput").ap()
        d3 = nc.dram_tensor("dbg_hs", [128, 16384], F32, kind="ExternalOutput").ap()
        d4 = nc.dram_tensor("dbg_w2t", [128, 1024], F32, kind="ExternalOutput").ap()
        p.wait(v, m, p.sem[m].n)
        p.wait(v, g, p.sem[g].n)
        p.wait(v, q, p.sem[q].n)
        p.wait(v, s, p.sem[s].n)
        p.wait(q, g, p.sem[g].n)
        p.wait(q, v, p.sem[v].n)
        p.wait(q, q, p.sem[q].n)
        for blk_i in range(4):
            cp = p.op(v, lambda e, blk_i=blk_i: e.tensor_copy(
                XT[0], abv(XNT0 + blk_i * 1024, XNT0 + (blk_i + 1) * 1024)), v)
            p.wait(g, v, cp)
            wr = p.dma(g, d1[:, blk_i * 1024:(blk_i + 1) * 1024], XT[0], g)
            p.wait(v, g, wr)
        ld = p.dma(q, abv(WSTG0, WSTG0 + 4096), w1t_s[0:128, :], q)
        p.wait(v, q, ld)
        for blk_i in range(4):
            cp = p.op(v, lambda e, blk_i=blk_i: e.tensor_copy(
                XT[0], abv(WSTG0 + blk_i * 1024, WSTG0 + (blk_i + 1) * 1024)), v)
            p.wait(g, v, cp)
            wr = p.dma(g, d2[:, blk_i * 1024:(blk_i + 1) * 1024], XT[0], g)
            p.wait(v, g, wr)
        for blk_i in range(16):
            cp = p.op(v, lambda e, blk_i=blk_i: e.tensor_copy(
                XT[0], abv(F_HS + blk_i * 1024, F_HS + (blk_i + 1) * 1024)), v)
            p.wait(g, v, cp)
            wr = p.dma(g, d3[:, blk_i * 1024:(blk_i + 1) * 1024], XT[0], g)
            p.wait(v, g, wr)
        ld = p.dma(q, abv(WSTG0, WSTG0 + 1024), w2t_s[0:128, :], q)
        p.wait(v, q, ld)
        cp = p.op(v, lambda e: e.tensor_copy(XT[0], abv(WSTG0, WSTG0 + 1024)), v)
        p.wait(g, v, cp)
        p.dma(g, d4[:, 0:1024], XT[0], g)

    p.wait(g, g, p.sem[g].n)
    p.wait(g, q, p.sem[q].n)
    p.wait(g, v, p.sem[v].n)
    p.wait(g, s, p.sem[s].n)
    p.wait(g, m, p.sem[m].n)

    with nc.Block() as blk:
        p.run(blk)
    return nc


def _host_inputs(core, full):
    d = {}
    x = full["L0_attn_a_x"]
    d["x_in"] = np.ascontiguousarray(x[2 * core:2 * core + 2].reshape(T, D))
    d["wqk"] = full["L0_attn_a_Wqk"]
    d["wv_a"] = full["L0_attn_a_Wv"]
    d["wn"] = full["L0_attn_a_Wn"]
    d["wo_a"] = full["L0_attn_a_Wo"]
    d["wq_b"] = np.ascontiguousarray(full["L0_attn_b_Wq"].transpose(1, 0, 2).reshape(D, D))
    d["wk_b"] = np.ascontiguousarray(full["L0_attn_b_Wk"].transpose(1, 0, 2).reshape(D, D))
    d["wv_b"] = np.ascontiguousarray(full["L0_attn_b_Wv"].transpose(1, 0, 2).reshape(D, D))
    d["wo_b"] = full["L0_attn_b_Wo"]
    d["w1"] = full["L0_ffn_w1"]
    d["w2"] = full["L0_ffn_w2"]
    d["bqk_pp"] = np.ascontiguousarray(full["L0_attn_a_bqk"].reshape(16, 128).T)
    d["bn_pp"] = np.ascontiguousarray(full["L0_attn_a_bn"].reshape(16, 1))
    d["bq_b_pp"] = np.ascontiguousarray(full["L0_attn_b_bq"].reshape(1024).reshape(8, 128).T)
    d["bk_b_pp"] = np.ascontiguousarray(full["L0_attn_b_bk"].reshape(1024).reshape(8, 128).T)
    d["b1_pp"] = np.ascontiguousarray(full["L0_ffn_b1"].reshape(32, 128).T)
    rows = np.zeros((128, 2048), np.float32)
    rows[0, 0:1024] = full["L0_attn_a_bv"]
    rows[0, 1024:2048] = full["L0_attn_a_bo"]
    rows[64, 0:1024] = full["L0_attn_b_bv"].reshape(1024)
    rows[64, 1024:2048] = full["L0_attn_b_bo"]
    d["rows_c"] = rows
    d["b2_row_f32"] = np.ascontiguousarray(full["L0_ffn_b2"].reshape(1, 1024))
    d["ident_c"] = np.eye(128, dtype=np.float32)
    jj = np.arange(896)[None, :]
    pp = np.arange(128)[:, None]
    d["cs_c"] = (jj >= pp + 384).astype(np.float32)
    esel = np.zeros((16, 1024), np.float32)
    for c in range(8):
        for pi in range(128):
            esel[2 * c + pi // 64, c * 128 + pi] = 1.0
    d["esel_c"] = esel
    om = np.zeros((128, 128), np.float32)
    om[64, 0:64] = 1.0
    om[32, 64:128] = 1.0
    d["omask_c"] = om
    cv = np.zeros((128, 8), np.float32)
    cv[:, 0] = 1e-5
    cv[:, 1] = -8.0
    d["cvals"] = cv
    return d


_NC_CACHE = {}


def kernel(**inputs):
    chain = ("attn_a", "ffn", "attn_b", "ffn")
    if chain not in _NC_CACHE:
        _NC_CACHE[chain] = build_nc(chain)
    nc = _NC_CACHE[chain]
    full = {k: np.asarray(val) for k, val in inputs.items()}
    in_maps = [_host_inputs(c, full) for c in range(8)]
    res = run_bass_kernel_spmd(nc, in_maps, core_ids=list(range(8)))
    out = np.concatenate([r["y"].reshape(2, S, D) for r in res.results], axis=0)
    return np.ascontiguousarray(out.astype(np.float32))
